# version 08
# baseline (speedup 1.0000x reference)
import numpy as np
from contextlib import ExitStack

import concourse.bass as bass
import concourse.mybir as mybir
from concourse.bass_utils import run_bass_kernel_spmd

F32 = mybir.dt.float32
BF16 = mybir.dt.bfloat16
F8 = mybir.dt.float8e4
AT = mybir.AluOpType
ACT = mybir.ActivationFunctionType
AX = mybir.AxisListType

D = 1024
H = 16
NC = 8
TILE = 512
NSUB = 4
EPS = 1e-5
N_CORES = 8
INV_D = 1.0 / D


class Prog:
    def __init__(self, nc, es):
        self.sems = {}
        self.cnt = {}
        for name in ("tp", "vp", "sp", "gp"):
            self.sems[name] = es.enter_context(nc.semaphore(name))
            self.cnt[name] = 0

    def inc(self, name, instr):
        instr.then_inc(self.sems[name], 1)
        self.cnt[name] += 1
        return self.cnt[name]

    def wait(self, eng, name, n=None):
        eng.wait_ge(self.sems[name], self.cnt[name] if n is None else n)


class Dma:
    def __init__(self, nc, es, name):
        self.sem = es.enter_context(nc.semaphore(name))
        self.n = 0

    def inc(self, instr):
        instr.then_inc(self.sem, 16)
        self.n += 16
        return self.n

    def wait(self, eng, n=None):
        eng.wait_ge(self.sem, self.n if n is None else n)


def build_nc(n_tiles, n_cores=N_CORES, stage=99):
    S = n_tiles * TILE
    NST = S // 128
    nc = bass.Bass("TRN2", target_bir_lowering=False, debug=False,
                   detect_race_conditions=False)

    x_hbm = nc.dram_tensor("x", [S, D], F32, kind="ExternalInput").ap()
    wqkv_hbm = nc.dram_tensor("wqkv", [D, 3 * D], F32, kind="ExternalInput").ap()
    wout_hbm = nc.dram_tensor("wout", [D, D], F32, kind="ExternalInput").ap()
    wffn_hbm = nc.dram_tensor("wffn", [D, D], F32, kind="ExternalInput").ap()
    bias_hbm = nc.dram_tensor("fbias", [1, D], F32, kind="ExternalInput").ap()
    c_hbm = nc.dram_tensor("c_row", [1, D], F32, kind="ExternalInput").ap()
    wmod_hbm = nc.dram_tensor("wmod", [D, 6 * D], F32, kind="ExternalInput").ap()
    cstf_hbm = nc.dram_tensor("cst_f", [128, 128], F32, kind="ExternalInput").ap()
    cstb_hbm = nc.dram_tensor("cst_b", [772, 128], BF16, kind="ExternalInput").ap()
    out_hbm = nc.dram_tensor("out", [S, D], F32, kind="ExternalOutput").ap()
    dbg_hbm = nc.dram_tensor("dbg", [128, 8192], F32, kind="ExternalOutput").ap()
    cc_in = nc.dram_tensor("cc_in", [1, 64], F32)
    cc_out = nc.dram_tensor("cc_out", [1, 64], F32)

    es = ExitStack()
    sbt = lambda name, shp, dt: es.enter_context(nc.sbuf_tensor(name, shp, dt))

    x_sb = sbt("x_sb", [128, NST * D], F32)
    scr_bf = sbt("scr_bf", [128, 4096], BF16)
    qz_sb = sbt("qz_sb", [128, 2048], BF16)
    ca1_fm = sbt("ca1_fm", [128, NC], F32)
    sa_fm = sbt("sa_fm", [128, NC], F32)
    cb1_fm = sbt("cb1_fm", [128, NC], F32)
    sb_fm = sbt("sb_fm", [128, NC], F32)
    ga_b = sbt("ga_b", [128, D], BF16)
    gbrow = sbt("gbrow", [1, D], F32)
    biasrow = sbt("biasrow", [1, D], F32)
    biasq = sbt("biasq", [1, D], BF16)
    silu_f = sbt("silu_f", [128, NC], F32)
    silu_b = sbt("silu_b", [128, NC], BF16)
    silu_lo = sbt("silu_lo", [128, NC], BF16)
    ident = sbt("ident", [128, 128], F32)
    identb = sbt("identb", [128, 128], BF16)
    maskp = sbt("maskp", [128, 128], BF16)
    maskd = sbt("maskd", [128, 128], BF16)
    maskadd = sbt("maskadd", [128, 256], BF16)
    bdiag = sbt("bdiag", [128, 128], BF16)
    ones1 = sbt("ones1", [1, 128], BF16)
    negones1 = sbt("negones1", [1, 128], BF16)
    kmpair = sbt("kmpair", [16, 2], F32)
    km_use = sbt("km_use", [16, 1], F32)
    kmrow_bf = sbt("kmrow_bf", [1, 16], BF16)
    kmb16 = sbt("kmb16", [128, 16], F32)
    km128 = sbt("km128", [128, NC], BF16)
    brow4 = sbt("brow4", [1, 512], BF16)
    row512 = sbt("row512", [1, 512], F32)
    rinv = sbt("rinv", [128, 16], F32)
    rtmp = sbt("rtmp", [128, 16], F32)
    sxa4 = sbt("sxa4", [128, 4], F32)
    ssq4 = sbt("ssq4", [128, 4], F32)
    mean4 = sbt("mean4", [128, 4], F32)
    msq4 = sbt("msq4", [128, 4], F32)
    var4 = sbt("var4", [128, 4], F32)
    aa4 = sbt("aa4", [128, 4], F32)
    bb4 = sbt("bb4", [128, 4], F32)
    lntmp4 = sbt("lntmp4", [128, 4], F32)
    amax_run = sbt("amax_run", [128, 1], F32)
    amax16 = sbt("amax16", [128, n_tiles], F32)
    flv = sbt("flv", [128, 1], F32)
    fls = sbt("fls", [128, 1], F32)
    row128 = sbt("row128", [1, 128], F32)
    rowb = sbt("rowb", [1, 128], F32)
    sc1 = sbt("sc1", [1, 1], F32)
    sc2 = sbt("sc2", [1, 1], F32)
    sc3 = sbt("sc3", [1, 1], F32)
    amaxw_s = sbt("amaxw_s", [1, 1], F32)
    swh128 = sbt("swh128", [128, 1], F32)
    sxh128 = sbt("sxh128", [128, 1], F32)
    eps_c = sbt("eps_c", [128, 1], F32)
    tiny_c = sbt("tiny_c", [1, 1], F32)
    onescol = sbt("onescol", [128, 1], BF16)
    dbg_fill = sbt("dbg_fill", [128, 64], F32)
    ccin_s = sbt("ccin_s", [1, 64], F32)
    ccout_s = sbt("ccout_s", [1, 64], F32)

    PA = es.enter_context(nc.psum_tensor("PA", [128, 1024], F32))
    PB = es.enter_context(nc.psum_tensor("PB", [128, 1024], F32))
    PC = es.enter_context(nc.psum_tensor("PC", [128, 1024], F32))
    PD = es.enter_context(nc.psum_tensor("PD", [128, 1024], F32))

    pg = Prog(nc, es)
    d_x = Dma(nc, es, "d_x")
    d_wg = Dma(nc, es, "d_wg")
    d_ws = Dma(nc, es, "d_ws")
    d_out = Dma(nc, es, "d_out")
    d_cc = Dma(nc, es, "d_cc")
    cc_sem = es.enter_context(nc.semaphore("cc_sem"))

    blk = es.enter_context(nc.Block())

    def on(engine, fn):
        getattr(blk, engine)(fn)

    def r3(ap, b):
        return ap.rearrange("p (a b) -> p a b", b=b)

    open_cms = []

    def cm_enter(cm):
        open_cms.append(cm)
        return cm.__enter__()

    def cm_exit(cm):
        assert open_cms and open_cms[-1] is cm
        open_cms.pop()
        cm.__exit__(None, None, None)

    def finish_early():
        def f_fin(sy):
            pg.wait(sy, "vp")
            pg.wait(sy, "sp")
            pg.wait(sy, "tp")
            pg.wait(sy, "gp")
            d_x.wait(sy)
            d_ws.wait(sy)
            d_wg.wait(sy)
            d_cc.wait(sy)
            for t in range(n_tiles):
                d_out.inc(sy.dma_start(
                    out=out_hbm.rearrange("(s p) d -> p s d", p=128)
                    [:, t * NSUB:(t + 1) * NSUB, :],
                    in_=x_sb[:, t * NSUB * D:(t + 1) * NSUB * D]))
            pg.wait(sy, "gp", 1)
            import os as _os2
            dumped = (int(_os2.environ.get("KDSU", "-1")) >= 0)
            lo = 8160 if dumped else 0
            for c0 in range(lo, 8192, 64):
                w = min(64, 8192 - c0)
                d_out.inc(sy.dma_start(out=dbg_hbm[:, c0:c0 + w],
                                       in_=dbg_fill[:, 0:w]))
            d_out.wait(sy)
        on("sync", f_fin)

    def cut():
        finish_early()
        while open_cms:
            open_cms.pop().__exit__(None, None, None)
        es.close()
        return nc

    wqkv_cm = nc.sbuf_tensor("wqkv_sb", [128, NC * 3 * D], BF16)
    wqkv_sb = cm_enter(wqkv_cm)
    wout_cm = nc.sbuf_tensor("wout_sb", [128, NC * D], BF16)
    wout_sb = cm_enter(wout_cm)
    wmgA_cm = nc.sbuf_tensor("wmgA", [128, NC * 256], F32)
    wmgA = cm_enter(wmgA_cm)
    wmgB_cm = nc.sbuf_tensor("wmgB", [128, NC * 256], F32)
    wmgB = cm_enter(wmgB_cm)
    garow_cm = nc.sbuf_tensor("garow", [1, 2 * D], F32)
    garow = cm_enter(garow_cm)
    gclamp_cm = nc.sbuf_tensor("gclamp", [1, 2 * D], F32)
    gclamp = cm_enter(gclamp_cm)
    rowbuf_cm = nc.sbuf_tensor("rowbuf", [1, 256], F32)
    rowbuf = cm_enter(rowbuf_cm)
    crow_cm = nc.sbuf_tensor("crow", [1, D], F32)
    crow = cm_enter(crow_cm)

    def f_sy0(sy):
        d_ws.inc(sy.dma_start(out=ident[:], in_=cstf_hbm))
        d_ws.inc(sy.dma_start(out=identb[:], in_=cstb_hbm[0:128, :]))
        d_ws.inc(sy.dma_start(out=maskp[:], in_=cstb_hbm[128:256, :]))
        d_ws.inc(sy.dma_start(out=maskd[:], in_=cstb_hbm[256:384, :]))
        d_ws.inc(sy.dma_start(out=bdiag[:], in_=cstb_hbm[384:512, :]))
        d_ws.inc(sy.dma_start(out=ones1[:], in_=cstb_hbm[512:513, :]))
        d_ws.inc(sy.dma_start(out=negones1[:], in_=cstb_hbm[513:514, :]))
        d_ws.inc(sy.dma_start(out=maskadd[:, 0:128], in_=cstb_hbm[516:644, :]))
        d_ws.inc(sy.dma_start(out=maskadd[:, 128:256], in_=cstb_hbm[644:772, :]))
        d_ws.inc(sy.dma_start(out=onescol[:],
                              in_=cstb_hbm[512:513, :].rearrange("a (p b) -> p (a b)", p=128)))
        d_ws.inc(sy.dma_start(out=crow[:], in_=c_hbm))
        d_ws.inc(sy.dma_start(out=biasrow[:], in_=bias_hbm))
        for t in range(n_tiles):
            d_x.inc(sy.dma_start(
                out=x_sb[:, t * NSUB * D:(t + 1) * NSUB * D],
                in_=x_hbm.rearrange("(s p) d -> p s d", p=128)
                [:, t * NSUB:(t + 1) * NSUB, :]))
    on("sync", f_sy0)

    def f_g0(g):
        g.memset(kmpair[:], 0.0)
        g.memset(dbg_fill[:], 0.0)
        g.memset(ccin_s[:], 0.0)
        g.memset(eps_c[:], EPS)
        g.memset(flv[:], 0.0)
        g.memset(fls[:], 0.0)
        g.memset(qz_sb[:], 0.0)
        li = g.memset(tiny_c[:], 1e-30)
        pg.inc("gp", li)
        d_wg.inc(g.dma_start(
            out=r3(wqkv_sb[:], 3 * D),
            in_=wqkv_hbm.rearrange("(c p) f -> p c f", p=128)))
        d_wg.inc(g.dma_start(
            out=r3(wout_sb[:], D),
            in_=wout_hbm.rearrange("(c p) f -> p c f", p=128)))
    on("gpsimd", f_g0)

    def f_t0(t):
        pg.wait(t, "gp", 1)
        d_ws.wait(t)
        for c in range(NC):
            li = t.transpose(PD[:, c:c + 1], crow[0:1, c * 128:(c + 1) * 128],
                             ident[0:1, 0:1])
        pg.inc("tp", li)
    on("tensor", f_t0)

    def f_s0(s):
        pg.wait(s, "tp")
        s.activation(silu_f[:, 0:NC], PD[:, 0:NC], ACT.Silu)
        s.copy(fls[:], eps_c[:])
        li = s.copy(fls[:], eps_c[:])
        pg.inc("sp", li)
    on("scalar", f_s0)

    def f_v0(v):
        pg.wait(v, "sp")
        v.tensor_copy(silu_b[:], silu_f[:])
        v.tensor_copy(flv[:], eps_c[:])
        v.tensor_copy(flv[:], eps_c[:])
        v.tensor_tensor(silu_lo[:], silu_f[:], silu_b[:], AT.subtract)
        v.tensor_copy(flv[:], eps_c[:])
        li = v.tensor_copy(flv[:], eps_c[:])
        pg.inc("vp", li)
    on("vector", f_v0)

    pieces = ([(cc, False) for cc in range(0, 2048, 256)] +
              [(cc, False) for cc in range(3072, 5120, 256)] +
              [(cc, True) for cc in range(2048, 3072, 256)] +
              [(cc, True) for cc in range(5120, 6144, 256)])
    import os as _os
    KADIS = int(_os.environ.get("KADIS", "9"))
    _np_pieces = int(_os.environ.get("KPIECES", str(len(pieces))))
    pieces = pieces[:_np_pieces]
    buf_last_use = {0: 0, 1: 0}
    for pi, (col0, is32) in enumerate(pieces):
        bi = pi % 2
        buf = (wmgA, wmgB)[bi]
        slot = PB[0:1, bi * 512: bi * 512 + 256]

        def f_syp(sy, buf=buf, bi=bi, col0=col0):
            if buf_last_use[bi]:
                pg.wait(sy, "tp", buf_last_use[bi])
            d_ws.inc(sy.dma_start(
                out=r3(buf[:], 256),
                in_=wmod_hbm.rearrange("(c p) f -> p c f", p=128)[:, :, col0:col0 + 256]))
        on("sync", f_syp)

        def f_vc(v, buf=buf, is32=is32):
            d_ws.wait(v)
            pg.wait(v, "tp")
            li = v.tensor_copy(scr_bf[:, 0:2048], buf[:])
            if is32:
                li = v.tensor_tensor(scr_bf[:, 2048:4096], buf[:],
                                     scr_bf[:, 0:2048], AT.subtract)
            pg.inc("vp", li)
        on("vector", f_vc)

        def f_tm(t, slot=slot, is32=is32, pi=pi, bi=bi):
            pg.wait(t, "vp")
            if pi >= 2:
                pg.wait(t, "sp")
            chains = [(silu_b, 0)]
            if is32:
                chains = [(silu_b, 0), (silu_b, 2048), (silu_lo, 0)]
            n_mm = len(chains) * NC
            k = 0
            for lh, off in chains:
                for c in range(NC):
                    k += 1
                    li = t.matmul(slot, lh[:, c:c + 1],
                                  scr_bf[:, off + c * 256: off + (c + 1) * 256],
                                  start=(k == 1), stop=(k == n_mm))
            buf_last_use[bi] = pg.inc("tp", li)
        on("tensor", f_tm)

        if is32:
            ga_off = (col0 - 2048) if col0 < 3072 else (col0 - 5120 + 1024)

            def f_sev(s, slot=slot, ga_off=ga_off):
                pg.wait(s, "tp")
                li = s.copy(garow[0:1, ga_off:ga_off + 256], slot)
                pg.inc("sp", li)
            on("scalar", f_sev)
        else:
            vec = col0 // D
            d0 = col0 % D

            def f_sev(s, slot=slot):
                pg.wait(s, "tp")
                s.copy(rowbuf[:], slot)
                s.copy(fls[:], eps_c[:])
                li = s.copy(fls[:], eps_c[:])
                pg.inc("sp", li)
            on("scalar", f_sev)

            def f_tr(t):
                pg.wait(t, "sp")
                for i in range(2):
                    li = t.transpose(PD[:, 32 + i:33 + i],
                                     rowbuf[0:1, i * 128:(i + 1) * 128],
                                     ident[0:1, 0:1])
                pg.inc("tp", li)
            on("tensor", f_tr)

            dst, add1 = {0: (sa_fm, False), 1: (ca1_fm, True),
                         3: (sb_fm, False), 4: (cb1_fm, True)}[vec]

            def f_sev2(s, dst=dst, add1=add1, d0=d0):
                pg.wait(s, "tp")
                if add1:
                    li = s.add(dst[:, d0 // 128: d0 // 128 + 2], PD[:, 32:34], 1.0)
                else:
                    li = s.copy(dst[:, d0 // 128: d0 // 128 + 2], PD[:, 32:34])
                pg.inc("sp", li)
            on("scalar", f_sev2)

    if _np_pieces < 24 or int(_os.environ.get("KCUT2", "0")) == 1:
        return cut()

    def f_s1(s):
        pg.wait(s, "sp")
        s.copy(fls[:], eps_c[:])
        s.copy(fls[:], eps_c[:])
        li = s.activation(gclamp[:], garow[:], ACT.Abs)
        pg.inc("sp", li)
    on("scalar", f_s1)

    def f_v1(v):
        pg.wait(v, "sp")
        li = v.tensor_scalar_max(gclamp[:], gclamp[:], 0.25)
        pg.inc("vp", li)
    on("vector", f_v1)

    def f_s1b(s):
        pg.wait(s, "vp")
        li = s.activation(garow[:], garow[:], ACT.Sign)
        pg.inc("sp", li)
    on("scalar", f_s1b)

    def f_v1b(v):
        pg.wait(v, "sp")
        v.tensor_mul(garow[:], garow[:], gclamp[:])
        v.tensor_copy(gbrow[:], garow[0:1, D:2 * D])
        li = v.tensor_copy(scr_bf[0:1, 0:1024], garow[0:1, 0:D])
        pg.inc("vp", li)
    on("vector", f_v1b)

    def f_t1(t):
        pg.wait(t, "vp")
        t.matmul(PA[:, 0:512], ones1[:], scr_bf[0:1, 0:512], start=True, stop=True)
        li = t.matmul(PA[:, 512:1024], ones1[:], scr_bf[0:1, 512:1024],
                      start=True, stop=True)
        pg.inc("tp", li)
    on("tensor", f_t1)

    def f_s1c(s):
        pg.wait(s, "tp")
        li = s.copy(ga_b[:], PA[:, 0:1024])
        pg.inc("sp", li)
    on("scalar", f_s1c)

    if stage <= 2:
        return cut()

    cm_exit(crow_cm)
    cm_exit(rowbuf_cm)
    cm_exit(gclamp_cm)
    cm_exit(garow_cm)
    cm_exit(wmgB_cm)
    cm_exit(wmgA_cm)

    q_cm = nc.sbuf_tensor("q_sb", [128, NC * TILE], BF16)
    q_sb = cm_enter(q_cm)
    k_cm = nc.sbuf_tensor("k_sb", [128, NC * TILE], BF16)
    k_sb = cm_enter(k_cm)
    kp_cm = nc.sbuf_tensor("kp_sb", [128, NC * 128], BF16)
    kp_sb = cm_enter(kp_cm)
    v_cm = nc.sbuf_tensor("v_sb", [128, NSUB * H * 65], BF16)
    v_sb = cm_enter(v_cm)
    vp_cm = nc.sbuf_tensor("vp_sb", [128, H * 65], BF16)
    vp_sb = cm_enter(vp_cm)
    h_cm = nc.sbuf_tensor("h_sb", [128, NC * TILE], BF16)
    h_sb = cm_enter(h_cm)
    pt_cm = nc.sbuf_tensor("pt_sb", [128, H * 256], BF16)
    pt_sb = cm_enter(pt_cm)

    at_view = pt_sb[:, 0:H * 64]
    af_view = pt_sb[:, 2048:2048 + NC * 128]
    scr32 = scr_bf[:].bitcast(F32)
    tr32 = scr32[:, 1024:2048]
    qz = qz_sb[:]


    def f_vones(v):
        d_ws.wait(v)
        li = v.tensor_copy(
            r3(v_sb[:], 65)[:, :, 64:65],
            onescol[:].rearrange("p (a b) -> p a b", b=1)
            .broadcast_to([128, NSUB * H, 1]))
        pg.inc("vp", li)
    on("vector", f_vones)

    def xs_ln(ti, su):
        return x_sb[:, (ti * NSUB + su) * D:(ti * NSUB + su + 1) * D]

    for ti in range(n_tiles):
        sp0 = pg.cnt["sp"]
        vp0 = pg.cnt["vp"]

        def f_sln(s, vp0=vp0, sp0=sp0, ti=ti):
            d_x.wait(s, 16 * (ti + 1))
            pg.wait(s, "tp")
            pg.wait(s, "vp")
            for su in range(NSUB):
                xs = x_sb[:, (ti * NSUB + su) * D:(ti * NSUB + su + 1) * D]
                li = s.activation(scr_bf[:, su * D:(su + 1) * D], xs, ACT.Square,
                                  accum_out=ssq4[:, su:su + 1])
            pg.inc("sp", li)
            pg.wait(s, "vp", vp0 + 1)
            for su in range(NSUB):
                s.activation(lntmp4[:, su:su + 1], var4[:, su:su + 1],
                             ACT.Ln, bias=eps_c[:])
            for su in range(NSUB):
                li = s.activation(aa4[:, su:su + 1], lntmp4[:, su:su + 1],
                                  ACT.Exp, scale=-0.5)
            pg.inc("sp", li)
        on("scalar", f_sln)

        def f_vln(v, ti=ti, sp0=sp0):
            d_x.wait(v, 16 * (ti + 1))
            for su in range(NSUB):
                xs = x_sb[:, (ti * NSUB + su) * D:(ti * NSUB + su + 1) * D]
                v.reduce_sum(sxa4[:, su:su + 1], xs, axis=AX.X)
            for su in range(NSUB):
                v.tensor_scalar_mul(mean4[:, su:su + 1], sxa4[:, su:su + 1], INV_D)
            for su in range(NSUB):
                v.tensor_tensor(msq4[:, su:su + 1], mean4[:, su:su + 1],
                                mean4[:, su:su + 1], AT.mult)
            pg.wait(v, "sp", sp0 + 1)
            for su in range(NSUB):
                li = v.tensor_scalar(var4[:, su:su + 1], ssq4[:, su:su + 1],
                                     INV_D, msq4[:, su:su + 1],
                                     AT.mult, AT.subtract)
            pg.inc("vp", li)
            pg.wait(v, "sp", sp0 + 2)
            for su in range(NSUB):
                v.tensor_scalar(bb4[:, su:su + 1], mean4[:, su:su + 1],
                                aa4[:, su:su + 1], -1.0, AT.mult, AT.mult)
            for su in range(NSUB):
                li = v.tensor_scalar(scr_bf[:, su * D:(su + 1) * D], xs_ln(ti, su),
                                     aa4[:, su:su + 1], bb4[:, su:su + 1],
                                     AT.mult, AT.add)
            pg.inc("vp", li)
        on("vector", f_vln)

        for rnd in range(4):
            reg = (PA, PB)[rnd % 2]

            def f_tt(t, rnd=rnd, reg=reg):
                pg.wait(t, "vp")
                pg.wait(t, "sp")
                regb = reg[:].bitcast(BF16)
                for cl, c in enumerate((2 * rnd, 2 * rnd + 1)):
                    for su in range(NSUB):
                        li = t.transpose(
                            regb[:, cl * 512 + su * 128: cl * 512 + (su + 1) * 128],
                            scr_bf[:, su * D + c * 128: su * D + (c + 1) * 128],
                            identb[:])
                pg.inc("tp", li)
            on("tensor", f_tt)

            def f_st(s, rnd=rnd, reg=reg):
                pg.wait(s, "tp")
                regb = reg[:].bitcast(BF16)
                for cl, c in enumerate((2 * rnd, 2 * rnd + 1)):
                    li = s.activation(h_sb[:, c * 512:(c + 1) * 512],
                                      regb[:, cl * 512:(cl + 1) * 512], ACT.Identity,
                                      bias=sa_fm[:, c:c + 1], scale=ca1_fm[:, c:c + 1])
                pg.inc("sp", li)
            on("scalar", f_st)

        for part, f0, dst, scale in (("q", 0, q_sb, 0.125), ("k", D, k_sb, None)):
            for fc in range(NC):
                half = (PA, PB)[fc % 2]

                def f_tqk(t, fc=fc, f0=f0, half=half, part=part, ti=ti):
                    if part == "q" and fc == 0:
                        if ti == 0:
                            d_wg.wait(t)
                    pg.wait(t, "sp")
                    pg.wait(t, "vp")
                    for dcc in range(NC):
                        li = t.matmul(half[:, 0:512],
                                      wqkv_sb[:, dcc * 3 * D + f0 + fc * 128:
                                              dcc * 3 * D + f0 + (fc + 1) * 128],
                                      h_sb[:, dcc * 512:(dcc + 1) * 512],
                                      start=(dcc == 0), stop=(dcc == NC - 1))
                    pg.inc("tp", li)
                on("tensor", f_tqk)

                if scale is not None:
                    def f_vqk(v, fc=fc, dst=dst, half=half, scale=scale):
                        pg.wait(v, "tp")
                        li = v.tensor_scalar(dst[:, fc * 512:(fc + 1) * 512],
                                             half[:, 0:512], scale, None, AT.mult)
                        pg.inc("vp", li)
                    on("vector", f_vqk)
                else:
                    def f_sqk(s, fc=fc, dst=dst, half=half):
                        pg.wait(s, "tp")
                        li = s.copy(dst[:, fc * 512:(fc + 1) * 512],
                                    half[:, 0:512])
                        pg.inc("sp", li)
                    on("scalar", f_sqk)

        for su in range(NSUB):
            half = (PA, PB)[su % 2]

            def f_tv(t, su=su, half=half):
                pg.wait(t, "sp")
                pg.wait(t, "vp")
                for dcc in range(NC):
                    lh = h_sb[:, dcc * 512 + su * 128: dcc * 512 + (su + 1) * 128]
                    t.matmul(half[:, 0:512], lh,
                             wqkv_sb[:, dcc * 3 * D + 2 * D: dcc * 3 * D + 2 * D + 512],
                             start=(dcc == 0), stop=(dcc == NC - 1))
                    li = t.matmul(half[:, 512:1024], lh,
                                  wqkv_sb[:, dcc * 3 * D + 2 * D + 512: dcc * 3 * D + 3 * D],
                                  start=(dcc == 0), stop=(dcc == NC - 1))
                pg.inc("tp", li)
            on("tensor", f_tv)

            def f_vv(v, su=su, half=half):
                pg.wait(v, "tp")
                li = v.tensor_copy(
                    r3(v_sb[:, su * H * 65:(su + 1) * H * 65], 65)[:, :, 0:64],
                    r3(half[:, 0:1024], 64))
                pg.inc("vp", li)
            on("vector", f_vv)

        def f_sk(s):
            pg.wait(s, "vp")
            s.activation(scr_bf[:, 0:2048], k_sb[:, 0:2048], ACT.Square)
            li = s.activation(scr_bf[:, 2048:4096], k_sb[:, 2048:4096], ACT.Square)
            pg.inc("sp", li)
        on("scalar", f_sk)

        def f_tk(t):
            pg.wait(t, "sp")
            for c in range(NC):
                li = t.matmul(PD[0:16, 0:512], bdiag[:, c * 16:(c + 1) * 16],
                              scr_bf[:, c * 512:(c + 1) * 512],
                              start=(c == 0), stop=(c == NC - 1))
            pg.inc("tp", li)
        on("tensor", f_tk)

        def f_vk(v, ti=ti):
            pg.wait(v, "tp")
            if ti == 0:
                pg.wait(v, "gp", 1)
            v.reduce_max(kmpair[:, 1:2], PD[0:16, 0:512], axis=AX.X)
            v.tensor_copy(flv[:], eps_c[:])
            v.tensor_copy(flv[:], eps_c[:])
            v.reduce_max(km_use[:], kmpair[:, 0:2], axis=AX.X)
            v.tensor_copy(kmpair[:, 0:1], kmpair[:, 1:2])
            li = v.tensor_copy(flv[:], eps_c[:])
            pg.inc("vp", li)
        on("vector", f_vk)

        def f_tk2(t):
            pg.wait(t, "vp")
            li = t.transpose(PD[0:1, 512:528], km_use[:], ident[0:16, 0:16])
            pg.inc("tp", li)
        on("tensor", f_tk2)

        def f_sk2(s):
            pg.wait(s, "tp")
            s.activation(kmrow_bf[:], PD[0:1, 512:528], ACT.Identity)
            s.copy(fls[:], eps_c[:])
            li = s.copy(fls[:], eps_c[:])
            pg.inc("sp", li)
        on("scalar", f_sk2)

        def f_tk4(t):
            pg.wait(t, "sp")
            li = t.matmul(PD[:, 528:544], ones1[:], kmrow_bf[:],
                          start=True, stop=True)
            pg.inc("tp", li)
        on("tensor", f_tk4)

        def f_vk3(v):
            pg.wait(v, "tp")
            v.tensor_copy(kmb16[:], PD[:, 528:544])
            v.tensor_copy(flv[:], eps_c[:])
            v.tensor_copy(km128[0:64, :], r3(kmb16[0:64, :], 2)[:, :, 0:1])
            v.tensor_copy(km128[64:128, :], r3(kmb16[64:128, :], 2)[:, :, 1:2])
            v.tensor_copy(flv[:], eps_c[:])
            li = v.tensor_copy(flv[:], eps_c[:])
            pg.inc("vp", li)
        on("vector", f_vk3)

        def f_sq4(s):
            pg.wait(s, "tp")
            pg.wait(s, "vp")
            li = s.activation(scr_bf[:, 0:4096], q_sb[:, 0:4096], ACT.Square)
            pg.inc("sp", li)
        on("scalar", f_sq4)

        def f_tb4(t):
            pg.wait(t, "vp")
            pg.wait(t, "sp")
            for su in range(NSUB):
                for c in range(NC):
                    li = t.matmul(PD[0:1, 512 + su * 128: 640 + su * 128],
                                  km128[:, c:c + 1],
                                  scr_bf[:, c * 512 + su * 128:
                                         c * 512 + (su + 1) * 128],
                                  start=(c == 0), stop=(c == NC - 1))
            pg.inc("tp", li)
        on("tensor", f_tb4)

        def f_sb24(s):
            pg.wait(s, "tp")
            s.activation(row512[:], PD[0:1, 512:1024], ACT.Ln, bias=tiny_c[:])
            s.copy(fls[:], eps_c[:])
            s.copy(fls[:], eps_c[:])
            s.activation(brow4[:], row512[:], ACT.Exp, scale=0.5)
            s.copy(fls[:], eps_c[:])
            li = s.copy(fls[:], eps_c[:])
            pg.inc("sp", li)
        on("scalar", f_sb24)

        for su in range(NSUB):
            if KADIS < 1:
                break
            first_global = (ti == 0 and su == 0)

            def f_vqz(v, su=su, ti=ti):
                pg.wait(v, "tp")
                if ti == 0 and su == 0:
                    pg.wait(v, "gp", 1)
                qz4 = qz.rearrange("p (a b c) -> p a b c", b=2, c=128)
                v.tensor_copy(qz4[0:64, :, 0, :],
                              r3(q_sb[0:64, :], 512)[:, :, su * 128:(su + 1) * 128])
                li = v.tensor_copy(qz4[64:128, :, 1, :],
                                   r3(q_sb[64:128, :], 512)[:, :, su * 128:(su + 1) * 128])
                pg.inc("vp", li)
            on("vector", f_vqz)

            if KADIS < 2:
                continue
            for g4 in range(4):
                reg = (PB, PC)[g4 % 2]

                def f_ts(t, g4=g4, reg=reg, su=su, first_global=first_global):
                    pg.wait(t, "sp")
                    pg.wait(t, "vp")
                    bspan = brow4[0:1, su * 128:(su + 1) * 128] \
                        .rearrange("p (a b c) -> p a b c", b=1, c=128) \
                        .broadcast_to([1, 2, 2, 128])
                    qz4 = qz.rearrange("p (a b c) -> p a b c", b=2, c=128)
                    madd = maskadd[:].rearrange("p (a b t) -> p a b t", a=1, b=2) \
                        .broadcast_to([128, 2, 2, 128])
                    for bank in range(2):
                        out4 = r3(reg[:, bank * 512:(bank + 1) * 512], 128) \
                            .rearrange("p (a b) c -> p a b c", b=2)
                        t.matmul(out4, identb[:], madd, start=True, stop=False)
                        t.matmul(out4, negones1[:], bspan, start=False, stop=False)
                        for hh in range(2):
                            h = g4 * 4 + bank * 2 + hh
                            par = h % 2
                            pr = h // 2
                            qrhs = qz4[:, pr, par, :]
                            col0 = bank * 512 + hh * 256
                            if not first_global:
                                if su == 0:
                                    klh = kp_sb[:, pr * 128:(pr + 1) * 128]
                                else:
                                    klh = k_sb[:, pr * 512 + (su - 1) * 128:
                                               pr * 512 + su * 128]
                                t.matmul(reg[:, col0:col0 + 128], klh, qrhs,
                                         start=False, stop=False)
                            li = t.matmul(reg[:, col0 + 128:col0 + 256],
                                          k_sb[:, pr * 512 + su * 128:
                                               pr * 512 + (su + 1) * 128],
                                          qrhs, start=False, stop=(hh == 1))
                    pg.inc("tp", li)
                on("tensor", f_ts)

                def f_se(s, g4=g4, reg=reg):
                    pg.wait(s, "tp")
                    li = s.activation(pt_sb[:, g4 * 1024:(g4 + 1) * 1024],
                                      reg[:], ACT.Exp)
                    pg.inc("sp", li)
                on("scalar", f_se)

            if KADIS < 3:
                continue

            def f_tpv(t, su=su, first_global=first_global):
                pg.wait(t, "sp")
                pg.wait(t, "vp")
                for rr in range(2):
                    reg = (PD, PC)[rr]
                    for hh in range(8):
                        h = rr * 8 + hh
                        outp = reg[:, hh * 128: hh * 128 + 65]
                        started = False
                        if not first_global:
                            if su == 0:
                                vrhs = vp_sb[:, h * 65:(h + 1) * 65]
                            else:
                                vrhs = v_sb[:, ((su - 1) * H + h) * 65:
                                            ((su - 1) * H + h + 1) * 65]
                            t.matmul(outp, pt_sb[:, h * 256: h * 256 + 128], vrhs,
                                     start=True, stop=False)
                            started = True
                        li = t.matmul(outp, pt_sb[:, h * 256 + 128: h * 256 + 256],
                                      v_sb[:, (su * H + h) * 65:(su * H + h + 1) * 65],
                                      start=not started, stop=True)
                pg.inc("tp", li)
            on("tensor", f_tpv)

            def f_vr(v):
                pg.wait(v, "tp")
                pd3 = r3(PD[:], 128)
                pc3 = r3(PC[:], 128)
                v.tensor_scalar_add(rtmp[:, 0:8], pd3[:, 0:8, 64:65], 1e-30)
                v.tensor_scalar_add(rtmp[:, 8:16], pc3[:, 0:8, 64:65], 1e-30)
                v.reciprocal(rinv[:, 0:8], rtmp[:, 0:8])
                v.reciprocal(rinv[:, 8:16], rtmp[:, 8:16])
                for rr, p3 in ((0, pd3), (1, pc3)):
                    li = v.tensor_tensor(
                        r3(at_view[:, rr * 512:(rr + 1) * 512], 64),
                        p3[:, 0:8, 0:64],
                        rinv[:, rr * 8:(rr + 1) * 8]
                        .rearrange("p (a b) -> p a b", b=1).broadcast_to([128, 8, 64]),
                        AT.mult)
                pg.inc("vp", li)
            on("vector", f_vr)

            if KADIS < 4:
                continue

            def f_ta(t):
                pg.wait(t, "vp")
                pbb = PB[:].bitcast(BF16)
                for c in range(NC):
                    li = t.transpose(pbb[:, c * 128:(c + 1) * 128],
                                     at_view[:, c * 128:(c + 1) * 128], identb[:])
                pg.inc("tp", li)
            on("tensor", f_ta)

            def f_saf(s):
                pg.wait(s, "tp")
                li = s.activation(af_view[:], PB[:].bitcast(BF16)[:, 0:1024],
                                  ACT.Identity)
                pg.inc("sp", li)
            on("scalar", f_saf)

            KDSU = int(_os.environ.get("KDSU", "-1"))
            if ti == 0 and su == KDSU:
                def f_dbg(g, su=su):
                    pg.wait(g, "vp")
                    pg.wait(g, "sp")
                    d_cc.inc(g.dma_start(out=dbg_hbm[:, 0:4096], in_=pt_sb[:]))
                    d_cc.inc(g.dma_start(out=dbg_hbm[:, 4096:4112], in_=rinv[:]))
                    d_cc.inc(g.dma_start(out=dbg_hbm[0:1, 4240:4368],
                                          in_=brow4[0:1, su * 128:(su + 1) * 128]))
                    d_cc.inc(g.dma_start(out=dbg_hbm[:, 4368:4376], in_=km128[:]))
                    d_cc.inc(g.dma_start(out=dbg_hbm[:, 4376:4392], in_=kmb16[:]))
                    d_cc.inc(g.dma_start(out=dbg_hbm[0:16, 4392:4393], in_=km_use[:], allow_slow_non_contiguous=True))
                    d_cc.inc(g.dma_start(out=dbg_hbm[0:16, 4393:4395], in_=kmpair[:], allow_slow_non_contiguous=True))
                    d_cc.inc(g.dma_start(out=dbg_hbm[:, 4416:5440],
                                          in_=scr_bf[:, 0:1024]))
                    d_cc.inc(g.dma_start(out=dbg_hbm[:, 5440:6464],
                                          in_=q_sb[:, 0:1024]))
                    d_cc.inc(g.dma_start(out=dbg_hbm[:, 6464:7488],
                                          in_=k_sb[:, 0:1024]))
                    d_cc.inc(g.dma_start(out=dbg_hbm[0:1, 7488:7489], in_=tiny_c[:]))
                    d_cc.inc(g.dma_start(out=dbg_hbm[:, 7489:7497],
                                          in_=silu_f[:, 0:8]))
                    d_cc.inc(g.dma_start(out=dbg_hbm[:, 7497:7505],
                                          in_=ca1_fm[:]))
                    d_cc.inc(g.dma_start(out=dbg_hbm[:, 7505:7513],
                                          in_=km128[:].bitcast(BF16)[:, 0:8]))
                    d_cc.inc(g.dma_start(out=dbg_hbm[:, 7520:8032],
                                          in_=h_sb[:, 0:512]))
                    d_cc.inc(g.dma_start(out=dbg_hbm[:, 8032:8096],
                                          in_=x_sb[:, 0:64]))
                    d_cc.inc(g.dma_start(out=dbg_hbm[:, 8096:8160],
                                          in_=wqkv_sb[:, 0:64]))
                on("gpsimd", f_dbg)

            if KADIS < 5:
                continue

            def f_tp(t):
                pg.wait(t, "sp")
                pg.wait(t, "vp")
                for dcc in range(NC):
                    lh = af_view[:, dcc * 128:(dcc + 1) * 128]
                    t.matmul(PA[:, 0:512], lh, wout_sb[:, dcc * D: dcc * D + 512],
                             start=(dcc == 0), stop=(dcc == NC - 1))
                    li = t.matmul(PA[:, 512:1024], lh,
                                  wout_sb[:, dcc * D + 512:(dcc + 1) * D],
                                  start=(dcc == 0), stop=(dcc == NC - 1))
                pg.inc("tp", li)
            on("tensor", f_tp)

            if KADIS < 6:
                continue

            def f_vres(v, ti=ti, su=su):
                pg.wait(v, "tp")
                xs = x_sb[:, (ti * NSUB + su) * D:(ti * NSUB + su + 1) * D]
                v.tensor_tensor(tr32, PA[:, 0:1024], ga_b[:], AT.mult)
                li = v.tensor_add(xs, xs, tr32)
                pg.inc("vp", li)
            on("vector", f_vres)

            if int(_os.environ.get("KONESU", "0")) and ti == 0 and su == 0:
                return cut()

        def f_vring(v):
            pg.wait(v, "sp")
            v.tensor_copy(r3(kp_sb[:], 128), r3(k_sb[:], 512)[:, :, 384:512])
            li = v.tensor_copy(vp_sb[:], v_sb[:, 3 * H * 65:4 * H * 65])
            pg.inc("vp", li)
        on("vector", f_vring)

    if stage <= 4:
        return cut()

    cm_exit(pt_cm)
    cm_exit(h_cm)
    cm_exit(vp_cm)
    cm_exit(v_cm)
    cm_exit(kp_cm)
    cm_exit(k_cm)
    cm_exit(q_cm)
    cm_exit(wout_cm)
    cm_exit(wqkv_cm)
    h2_cm = nc.sbuf_tensor("h2_sb", [128, NC * S], BF16)
    h2_sb = cm_enter(h2_cm)

    a2_tp = pg.cnt["tp"]
    a2_vp = pg.cnt["vp"]
    a2_sp = pg.cnt["sp"]

    for ti in range(n_tiles):
        sp0 = pg.cnt["sp"]
        vp0 = pg.cnt["vp"]

        def f_sln2(s, vp0=vp0, sp0=sp0, ti=ti):
            pg.wait(s, "tp")
            if ti == 0:
                pg.wait(s, "vp")
            for su in range(NSUB):
                xs = x_sb[:, (ti * NSUB + su) * D:(ti * NSUB + su + 1) * D]
                li = s.activation(scr_bf[:, su * D:(su + 1) * D], xs, ACT.Square,
                                  accum_out=ssq4[:, su:su + 1])
            pg.inc("sp", li)
            pg.wait(s, "vp", vp0 + 1)
            for su in range(NSUB):
                s.activation(lntmp4[:, su:su + 1], var4[:, su:su + 1],
                             ACT.Ln, bias=eps_c[:])
            for su in range(NSUB):
                li = s.activation(aa4[:, su:su + 1], lntmp4[:, su:su + 1],
                                  ACT.Exp, scale=-0.5)
            pg.inc("sp", li)
        on("scalar", f_sln2)

        def f_vln2(v, ti=ti, sp0=sp0):
            for su in range(NSUB):
                xs = x_sb[:, (ti * NSUB + su) * D:(ti * NSUB + su + 1) * D]
                v.reduce_sum(sxa4[:, su:su + 1], xs, axis=AX.X)
            for su in range(NSUB):
                v.tensor_scalar_mul(mean4[:, su:su + 1], sxa4[:, su:su + 1], INV_D)
            for su in range(NSUB):
                v.tensor_tensor(msq4[:, su:su + 1], mean4[:, su:su + 1],
                                mean4[:, su:su + 1], AT.mult)
            pg.wait(v, "sp", sp0 + 1)
            for su in range(NSUB):
                li = v.tensor_scalar(var4[:, su:su + 1], ssq4[:, su:su + 1],
                                     INV_D, msq4[:, su:su + 1],
                                     AT.mult, AT.subtract)
            pg.inc("vp", li)
            pg.wait(v, "sp", sp0 + 2)
            for su in range(NSUB):
                v.tensor_scalar(bb4[:, su:su + 1], mean4[:, su:su + 1],
                                aa4[:, su:su + 1], -1.0, AT.mult, AT.mult)
            for su in range(NSUB):
                li = v.tensor_scalar(scr_bf[:, su * D:(su + 1) * D], xs_ln(ti, su),
                                     aa4[:, su:su + 1], bb4[:, su:su + 1],
                                     AT.mult, AT.add)
            pg.inc("vp", li)
        on("vector", f_vln2)

        for rnd in range(4):
            reg = (PA, PB)[rnd % 2]

            def f_tt2(t, rnd=rnd, reg=reg):
                pg.wait(t, "vp")
                pg.wait(t, "sp")
                regb = reg[:].bitcast(BF16)
                for cl, c in enumerate((2 * rnd, 2 * rnd + 1)):
                    for su in range(NSUB):
                        li = t.transpose(
                            regb[:, cl * 512 + su * 128: cl * 512 + (su + 1) * 128],
                            scr_bf[:, su * D + c * 128: su * D + (c + 1) * 128],
                            identb[:])
                pg.inc("tp", li)
            on("tensor", f_tt2)

            def f_st2(s, rnd=rnd, reg=reg, ti=ti):
                pg.wait(s, "tp")
                regb = reg[:].bitcast(BF16)
                for cl, c in enumerate((2 * rnd, 2 * rnd + 1)):
                    li = s.activation(h2_sb[:, c * S + ti * 512: c * S + (ti + 1) * 512],
                                      regb[:, cl * 512:(cl + 1) * 512], ACT.Identity,
                                      bias=sb_fm[:, c:c + 1], scale=cb1_fm[:, c:c + 1])
                pg.inc("sp", li)
            on("scalar", f_st2)

        def f_vam(v, ti=ti):
            pg.wait(v, "sp")
            li = v.reduce_max(amax16[:, ti:ti + 1],
                              h2_sb[:].rearrange("p (c t) -> p c t", t=S)
                              [:, :, ti * 512:(ti + 1) * 512],
                              axis=AX.XY, apply_absolute_value=True)
            pg.inc("vp", li)
        on("vector", f_vam)

    wq8_cm = nc.sbuf_tensor("wq8_sb", [128, NC * D], F8)
    wq8_sb = cm_enter(wq8_cm)
    wst_cm = nc.sbuf_tensor("wst", [128, NC * D], F32)
    wst = cm_enter(wst_cm)

    def f_sy2(sy):
        pg.wait(sy, "tp", a2_tp)
        pg.wait(sy, "vp", a2_vp)
        pg.wait(sy, "sp", a2_sp)
        d_ws.inc(sy.dma_start(
            out=r3(wst[:], D),
            in_=wffn_hbm.rearrange("(c p) f -> p c f", p=128)))
    on("sync", f_sy2)

    def f_v2(v):
        d_ws.wait(v)
        v.reduce_max(kmb16[:, 0:1], r3(wst[:], D), axis=AX.XY,
                     apply_absolute_value=True)
        v.tensor_copy(flv[:], eps_c[:])
        li = v.tensor_copy(flv[:], eps_c[:])
        pg.inc("vp", li)
    on("vector", f_v2)

    def f_t2(t):
        pg.wait(t, "vp")
        li = t.transpose(PD[0:1, 48:176], kmb16[:, 0:1], ident[:])
        pg.inc("tp", li)
    on("tensor", f_t2)

    def f_s2(s):
        pg.wait(s, "tp")
        s.copy(row128[:], PD[0:1, 48:176])
        s.copy(fls[:], eps_c[:])
        li = s.copy(fls[:], eps_c[:])
        pg.inc("sp", li)
    on("scalar", f_s2)

    def f_v2b(v):
        pg.wait(v, "sp")
        v.reduce_max(amaxw_s[:], row128[:], axis=AX.X)
        v.tensor_copy(flv[:], eps_c[:])
        v.tensor_copy(flv[:], eps_c[:])
        v.tensor_scalar_max(amaxw_s[:], amaxw_s[:], 1e-12)
        v.tensor_copy(flv[:], eps_c[:])
        v.tensor_copy(flv[:], eps_c[:])
        v.reciprocal(sc1[:], amaxw_s[:])
        v.tensor_copy(flv[:], eps_c[:])
        v.tensor_copy(flv[:], eps_c[:])
        v.tensor_scalar_mul(sc1[:], sc1[:], 224.0)
        v.tensor_copy(flv[:], eps_c[:])
        v.tensor_copy(flv[:], eps_c[:])
        v.tensor_copy(rowb[:], sc1[0:1, 0:1].rearrange("p (a b) -> p a b", b=1)
                      .broadcast_to([1, 128, 1]))
        v.tensor_copy(flv[:], eps_c[:])
        li = v.tensor_copy(flv[:], eps_c[:])
        pg.inc("vp", li)
    on("vector", f_v2b)

    def f_t2b(t):
        pg.wait(t, "vp")
        li = t.transpose(PD[:, 176:177], rowb[0:1, :], ident[0:1, 0:1])
        pg.inc("tp", li)
    on("tensor", f_t2b)

    def f_s2b(s):
        pg.wait(s, "tp")
        s.copy(swh128[:], PD[:, 176:177])
        s.copy(fls[:], eps_c[:])
        li = s.copy(fls[:], eps_c[:])
        pg.inc("sp", li)
    on("scalar", f_s2b)

    for dc in range(NC):
        def f_tq(t, dc=dc):
            pg.wait(t, "sp")
            pg.wait(t, "vp")
            for oc in range(NC):
                li = t.transpose(PA[:, oc * 128:(oc + 1) * 128],
                                 wst[:, oc * D + dc * 128: oc * D + (dc + 1) * 128],
                                 ident[:])
            pg.inc("tp", li)
        on("tensor", f_tq)

        def f_vq(v, dc=dc):
            pg.wait(v, "tp")
            li = v.tensor_scalar(wq8_sb[:, dc * D:(dc + 1) * D], PA[:, 0:1024],
                                 swh128[:], None, AT.mult)
            pg.inc("vp", li)
        on("vector", f_vq)

    cm_exit(wst_cm)

    if stage <= 3:
        return cut()


    def f_vamf(v):
        v.tensor_copy(flv[:], eps_c[:])
        v.tensor_copy(flv[:], eps_c[:])
        v.reduce_max(amax_run[:], amax16[:], axis=AX.X)
        v.tensor_copy(flv[:], eps_c[:])
        li = v.tensor_copy(flv[:], eps_c[:])
        pg.inc("vp", li)
    on("vector", f_vamf)

    def f_tam(t):
        pg.wait(t, "vp")
        li = t.transpose(PD[0:1, 768:896], amax_run[:], ident[:])
        pg.inc("tp", li)
    on("tensor", f_tam)

    def f_sam(s):
        pg.wait(s, "tp")
        s.copy(row128[:], PD[0:1, 768:896])
        s.copy(fls[:], eps_c[:])
        li = s.copy(fls[:], eps_c[:])
        pg.inc("sp", li)
    on("scalar", f_sam)

    def f_vam2(v):
        pg.wait(v, "sp")
        v.reduce_max(sc1[:], row128[:], axis=AX.X)
        v.tensor_copy(flv[:], eps_c[:])
        v.tensor_copy(flv[:], eps_c[:])
        v.tensor_scalar_max(sc1[:], sc1[:], 1e-12)
        v.tensor_copy(flv[:], eps_c[:])
        v.tensor_copy(flv[:], eps_c[:])
        v.tensor_copy(ccin_s[0:1, 0:1], sc1[:])
        v.tensor_copy(flv[:], eps_c[:])
        li = v.tensor_copy(flv[:], eps_c[:])
        pg.inc("vp", li)
    on("vector", f_vam2)

    def f_sycc(sy):
        pg.wait(sy, "vp")
        d_cc.inc(sy.dma_start(out=cc_in.ap(), in_=ccin_s[:]))
    on("sync", f_sycc)

    def f_gcc(g):
        d_cc.wait(g)
        g.collective_compute(
            "AllReduce", AT.max, replica_groups=[list(range(n_cores))],
            ins=[cc_in.ap().opt()], outs=[cc_out.ap().opt()]).then_inc(cc_sem, 1)
    on("gpsimd", f_gcc)

    def f_sycc2(sy):
        sy.wait_ge(cc_sem, 1)
        d_cc.inc(sy.dma_start(out=ccout_s[:], in_=cc_out.ap()))
    on("sync", f_sycc2)

    if stage <= 5:
        return cut()

    g1_cm = nc.sbuf_tensor("g1_b", [128, D], BF16)
    g1_b = cm_enter(g1_cm)
    xq_cm = nc.sbuf_tensor("xq_sb", [128, NC * TILE], F8)
    xq_sb = cm_enter(xq_cm)
    gbc_cm = nc.sbuf_tensor("gbc_row", [1, D], BF16)
    gbc_row = cm_enter(gbc_cm)

    def f_vg(v):
        d_cc.wait(v)
        v.reciprocal(sc2[:], ccout_s[0:1, 0:1])
        v.tensor_copy(flv[:], eps_c[:])
        v.tensor_copy(flv[:], eps_c[:])
        v.tensor_scalar_mul(sc2[:], sc2[:], 224.0)
        v.tensor_copy(flv[:], eps_c[:])
        v.tensor_copy(flv[:], eps_c[:])
        v.tensor_copy(rowb[:], sc2[0:1, 0:1].rearrange("p (a b) -> p a b", b=1)
                      .broadcast_to([1, 128, 1]))
        v.tensor_mul(sc2[:], ccout_s[0:1, 0:1], amaxw_s[:])
        v.tensor_copy(flv[:], eps_c[:])
        v.tensor_copy(flv[:], eps_c[:])
        v.tensor_scalar_mul(sc2[:], sc2[:], 1.0 / 50176.0)
        v.tensor_copy(flv[:], eps_c[:])
        v.tensor_copy(flv[:], eps_c[:])
        v.reciprocal(sc3[:], sc2[:])
        v.tensor_copy(flv[:], eps_c[:])
        v.tensor_copy(flv[:], eps_c[:])
        v.tensor_scalar(biasq[:], biasrow[:], sc3[:], None, AT.mult)
        li = v.tensor_scalar(gbc_row[:], gbrow[:], sc2[:], None, AT.mult)
        pg.inc("vp", li)
    on("vector", f_vg)

    def f_tg(t):
        pg.wait(t, "vp")
        t.transpose(PD[:, 896:897], rowb[0:1, :], ident[0:1, 0:1])
        t.matmul(PA[:, 0:512], ones1[:], gbc_row[0:1, 0:512], start=True, stop=True)
        li = t.matmul(PA[:, 512:1024], ones1[:], gbc_row[0:1, 512:1024],
                      start=True, stop=True)
        pg.inc("tp", li)
    on("tensor", f_tg)

    def f_sg(s):
        pg.wait(s, "tp")
        s.copy(sxh128[:], PD[:, 896:897])
        li = s.copy(g1_b[:], PA[:, 0:1024])
        pg.inc("sp", li)
    on("scalar", f_sg)

    for ti in range(n_tiles):
        def f_vxq(v, ti=ti):
            pg.wait(v, "vp")
            pg.wait(v, "sp")
            li = v.tensor_scalar(
                r3(xq_sb[:], 512),
                h2_sb[:].rearrange("p (c t) -> p c t", t=S)
                [:, :, ti * 512:(ti + 1) * 512],
                sxh128[:], None, AT.mult)
            pg.inc("vp", li)
        on("vector", f_vxq)

        vpB = pg.cnt["vp"]
        for su in range(NSUB):
            regf = (PA, PB)[su % 2]

            def f_tf(t, su=su, regf=regf, vpB=vpB):
                pg.wait(t, "vp", vpB + max(0, su - 1))
                t.matmul(regf[:, 0:512], ones1[:], biasq[0:1, 0:512],
                         start=True, stop=False)
                t.matmul(regf[:, 512:1024], ones1[:], biasq[0:1, 512:1024],
                         start=True, stop=False)
                for dcc in range(NC):
                    lh = xq_sb[:, dcc * 512 + su * 128: dcc * 512 + (su + 1) * 128]
                    t.matmul(regf[:, 0:512], lh, wq8_sb[:, dcc * D: dcc * D + 512],
                             start=False, stop=(dcc == NC - 1))
                    li = t.matmul(regf[:, 512:1024], lh,
                                  wq8_sb[:, dcc * D + 512:(dcc + 1) * D],
                                  start=False, stop=(dcc == NC - 1))
                pg.inc("tp", li)
            on("tensor", f_tf)

            def f_vf(v, ti=ti, su=su, regf=regf):
                pg.wait(v, "tp")
                xs = x_sb[:, (ti * NSUB + su) * D:(ti * NSUB + su + 1) * D]
                v.tensor_tensor(tr32, regf[:, 0:1024], g1_b[:], AT.mult)
                li = v.tensor_add(xs, xs, tr32)
                pg.inc("vp", li)
            on("vector", f_vf)

        def f_syo(sy, ti=ti):
            pg.wait(sy, "vp")
            d_out.inc(sy.dma_start(
                out=out_hbm.rearrange("(s p) d -> p s d", p=128)
                [:, ti * NSUB:(ti + 1) * NSUB, :],
                in_=x_sb[:, ti * NSUB * D:(ti + 1) * NSUB * D]))
        on("sync", f_syo)

    def f_syend(sy):
        for c0 in range(0, 8192, 64):
            d_out.inc(sy.dma_start(out=dbg_hbm[:, c0:c0 + 64], in_=dbg_fill[:])) \
                if False else None
        d_out.wait(sy)
    on("sync", f_syend)

    cm_exit(gbc_cm)
    cm_exit(xq_cm)
    cm_exit(g1_cm)
    cm_exit(wq8_cm)
    cm_exit(h2_cm)
    es.close()
    return nc


def _make_consts():
    import ml_dtypes
    cst_f = np.eye(128, dtype=np.float32)
    t = np.arange(128)
    p = np.arange(128)[:, None]
    rows = [np.eye(128, dtype=np.float32),
            (t[None, :] < p).astype(np.float32),
            (t[None, :] >= p).astype(np.float32)]
    bd = np.zeros((128, 128), dtype=np.float32)
    for c in range(NC):
        bd[0:64, c * 16 + 2 * c] = 1.0
        bd[64:128, c * 16 + 2 * c + 1] = 1.0
    rows.append(bd)
    ones = np.zeros((4, 128), dtype=np.float32)
    ones[0, :] = 1.0
    ones[1, :] = -1.0
    rows.append(ones)
    cst_b = np.concatenate(rows, axis=0)[0:516]
    mka = (1.0 - rows[1]) * -30000.0
    mkd = (1.0 - rows[2]) * -30000.0
    cst_b = np.concatenate([cst_b, mka, mkd], axis=0)
    return cst_f, np.ascontiguousarray(cst_b.astype(ml_dtypes.bfloat16))


_NC_CACHE = {}


def kernel(L0_attn_x, L0_attn_w_qkv, L0_attn_w_out, L0_ffn_weight, L0_ffn_bias,
           L0_c, L0_wmod):
    import os as _os
    stage = int(_os.environ.get("KSTAGE", "99"))
    n_tiles = L0_attn_x.shape[1] // TILE
    n_cores = L0_attn_x.shape[0]
    key = (n_tiles, n_cores, stage)
    if key not in _NC_CACHE:
        _NC_CACHE[key] = build_nc(n_tiles, n_cores, stage)
    nc = _NC_CACHE[key]

    x = np.ascontiguousarray(np.asarray(L0_attn_x, dtype=np.float32))
    wqkv = np.ascontiguousarray(np.asarray(L0_attn_w_qkv, dtype=np.float32))
    wout = np.ascontiguousarray(np.asarray(L0_attn_w_out, dtype=np.float32))
    wffn = np.ascontiguousarray(np.asarray(L0_ffn_weight, dtype=np.float32))
    fbias = np.ascontiguousarray(
        np.asarray(L0_ffn_bias, dtype=np.float32).reshape(1, D))
    c_all = np.ascontiguousarray(np.asarray(L0_c, dtype=np.float32))
    wmod = np.ascontiguousarray(np.asarray(L0_wmod, dtype=np.float32))
    cst_f, cst_b = _make_consts()

    in_maps = []
    for r in range(n_cores):
        in_maps.append({
            "x": np.ascontiguousarray(x[r]),
            "wqkv": wqkv,
            "wout": wout,
            "wffn": wffn,
            "fbias": fbias,
            "c_row": np.ascontiguousarray(c_all[r:r + 1]),
            "wmod": wmod,
            "cst_f": cst_f,
            "cst_b": cst_b,
        })
    res = run_bass_kernel_spmd(nc, in_maps, core_ids=list(range(n_cores)))
    out = np.stack([np.asarray(res.results[r]["out"])
                    for r in range(n_cores)], axis=0)
    return out


# version 09
# speedup vs baseline: 1.0975x; 1.0975x over previous
import numpy as np
from contextlib import ExitStack

import concourse.bass as bass
import concourse.mybir as mybir
from concourse.bass_utils import run_bass_kernel_spmd

F32 = mybir.dt.float32
BF16 = mybir.dt.bfloat16
F8 = mybir.dt.float8e4
AT = mybir.AluOpType
ACT = mybir.ActivationFunctionType
AX = mybir.AxisListType

D = 1024
H = 16
NC = 8
TILE = 512
NSUB = 4
EPS = 1e-5
N_CORES = 8
INV_D = 1.0 / D


class Prog:
    def __init__(self, nc, es):
        self.sems = {}
        self.cnt = {}
        for name in ("tp", "vp", "sp", "gp"):
            self.sems[name] = es.enter_context(nc.semaphore(name))
            self.cnt[name] = 0

    def inc(self, name, instr):
        instr.then_inc(self.sems[name], 1)
        self.cnt[name] += 1
        return self.cnt[name]

    def wait(self, eng, name, n=None):
        eng.wait_ge(self.sems[name], self.cnt[name] if n is None else n)


class Dma:
    def __init__(self, nc, es, name):
        self.sem = es.enter_context(nc.semaphore(name))
        self.n = 0

    def inc(self, instr):
        instr.then_inc(self.sem, 16)
        self.n += 16
        return self.n

    def wait(self, eng, n=None):
        eng.wait_ge(self.sem, self.n if n is None else n)


def build_nc(n_tiles, n_cores=N_CORES, stage=99):
    S = n_tiles * TILE
    NST = S // 128
    nc = bass.Bass("TRN2", target_bir_lowering=False, debug=False,
                   detect_race_conditions=False)

    x_hbm = nc.dram_tensor("x", [S, D], F32, kind="ExternalInput").ap()
    wqkv_hbm = nc.dram_tensor("wqkv", [D, 3 * D], F32, kind="ExternalInput").ap()
    wout_hbm = nc.dram_tensor("wout", [D, D], F32, kind="ExternalInput").ap()
    wffn_hbm = nc.dram_tensor("wffn", [D, D], F32, kind="ExternalInput").ap()
    bias_hbm = nc.dram_tensor("fbias", [1, D], F32, kind="ExternalInput").ap()
    c_hbm = nc.dram_tensor("c_row", [1, D], F32, kind="ExternalInput").ap()
    wmod_hbm = nc.dram_tensor("wmod", [D, 6 * D], F32, kind="ExternalInput").ap()
    cstf_hbm = nc.dram_tensor("cst_f", [128, 128], F32, kind="ExternalInput").ap()
    cstb_hbm = nc.dram_tensor("cst_b", [772, 128], BF16, kind="ExternalInput").ap()
    out_hbm = nc.dram_tensor("out", [S, D], F32, kind="ExternalOutput").ap()
    dbg_hbm = nc.dram_tensor("dbg", [128, 8192], F32, kind="ExternalOutput").ap()
    cc_in = nc.dram_tensor("cc_in", [1, 64], F32)
    cc_out = nc.dram_tensor("cc_out", [1, 64], F32)

    es = ExitStack()
    sbt = lambda name, shp, dt: es.enter_context(nc.sbuf_tensor(name, shp, dt))

    x_sb = sbt("x_sb", [128, NST * D], F32)
    scr_bf = sbt("scr_bf", [128, 4096], BF16)
    qz_sb = sbt("qz_sb", [128, 2048], BF16)
    ca1_fm = sbt("ca1_fm", [128, NC], F32)
    sa_fm = sbt("sa_fm", [128, NC], F32)
    cb1_fm = sbt("cb1_fm", [128, NC], F32)
    sb_fm = sbt("sb_fm", [128, NC], F32)
    ga_b = sbt("ga_b", [128, D], BF16)
    gbrow = sbt("gbrow", [1, D], F32)
    biasrow = sbt("biasrow", [1, D], F32)
    biasq = sbt("biasq", [1, D], BF16)
    silu_f = sbt("silu_f", [128, NC], F32)
    silu_b = sbt("silu_b", [128, NC], BF16)
    silu_lo = sbt("silu_lo", [128, NC], BF16)
    ident = sbt("ident", [128, 128], F32)
    identb = sbt("identb", [128, 128], BF16)
    maskp = sbt("maskp", [128, 128], BF16)
    maskd = sbt("maskd", [128, 128], BF16)
    maskadd = sbt("maskadd", [128, 256], BF16)
    bdiag = sbt("bdiag", [128, 128], BF16)
    ones1 = sbt("ones1", [1, 128], BF16)
    negones1 = sbt("negones1", [1, 128], BF16)
    kmpair = sbt("kmpair", [16, 2], F32)
    km_use = sbt("km_use", [16, 1], F32)
    kmrow_bf = sbt("kmrow_bf", [1, 16], BF16)
    kmb16 = sbt("kmb16", [128, 16], F32)
    km128 = sbt("km128", [128, NC], BF16)
    brow4 = sbt("brow4", [1, 512], BF16)
    row512 = sbt("row512", [1, 512], F32)
    rinv = sbt("rinv", [128, 16], F32)
    rtmp = sbt("rtmp", [128, 16], F32)
    sxa4 = sbt("sxa4", [128, 4], F32)
    ssq4 = sbt("ssq4", [128, 4], F32)
    mean4 = sbt("mean4", [128, 4], F32)
    msq4 = sbt("msq4", [128, 4], F32)
    var4 = sbt("var4", [128, 4], F32)
    aa4 = sbt("aa4", [128, 4], F32)
    bb4 = sbt("bb4", [128, 4], F32)
    lntmp4 = sbt("lntmp4", [128, 4], F32)
    amax_run = sbt("amax_run", [128, 1], F32)
    amax16 = sbt("amax16", [128, n_tiles], F32)
    flv = sbt("flv", [128, 1], F32)
    fls = sbt("fls", [128, 1], F32)
    row128 = sbt("row128", [1, 128], F32)
    rowb = sbt("rowb", [1, 128], F32)
    sc1 = sbt("sc1", [1, 1], F32)
    sc2 = sbt("sc2", [1, 1], F32)
    sc3 = sbt("sc3", [1, 1], F32)
    amaxw_s = sbt("amaxw_s", [1, 1], F32)
    swh128 = sbt("swh128", [128, 1], F32)
    sxh128 = sbt("sxh128", [128, 1], F32)
    eps_c = sbt("eps_c", [128, 1], F32)
    tiny_c = sbt("tiny_c", [1, 1], F32)
    onescol = sbt("onescol", [128, 1], BF16)
    dbg_fill = sbt("dbg_fill", [128, 64], F32)
    ccin_s = sbt("ccin_s", [1, 64], F32)
    ccout_s = sbt("ccout_s", [1, 64], F32)

    PA = es.enter_context(nc.psum_tensor("PA", [128, 1024], F32))
    PB = es.enter_context(nc.psum_tensor("PB", [128, 1024], F32))
    PC = es.enter_context(nc.psum_tensor("PC", [128, 1024], F32))
    PD = es.enter_context(nc.psum_tensor("PD", [128, 1024], F32))

    pg = Prog(nc, es)
    d_x = Dma(nc, es, "d_x")
    d_wg = Dma(nc, es, "d_wg")
    d_ws = Dma(nc, es, "d_ws")
    d_out = Dma(nc, es, "d_out")
    d_cc = Dma(nc, es, "d_cc")
    cc_sem = es.enter_context(nc.semaphore("cc_sem"))

    blk = es.enter_context(nc.Block())

    def on(engine, fn):
        getattr(blk, engine)(fn)

    def r3(ap, b):
        return ap.rearrange("p (a b) -> p a b", b=b)

    open_cms = []

    def cm_enter(cm):
        open_cms.append(cm)
        return cm.__enter__()

    def cm_exit(cm):
        assert open_cms and open_cms[-1] is cm
        open_cms.pop()
        cm.__exit__(None, None, None)

    def finish_early():
        def f_fin(sy):
            pg.wait(sy, "vp")
            pg.wait(sy, "sp")
            pg.wait(sy, "tp")
            pg.wait(sy, "gp")
            d_x.wait(sy)
            d_ws.wait(sy)
            d_wg.wait(sy)
            d_cc.wait(sy)
            for t in range(n_tiles):
                d_out.inc(sy.dma_start(
                    out=out_hbm.rearrange("(s p) d -> p s d", p=128)
                    [:, t * NSUB:(t + 1) * NSUB, :],
                    in_=x_sb[:, t * NSUB * D:(t + 1) * NSUB * D]))
            pg.wait(sy, "gp", 1)
            import os as _os2
            dumped = (int(_os2.environ.get("KDSU", "-1")) >= 0)
            lo = 8160 if dumped else 0
            for c0 in range(lo, 8192, 64):
                w = min(64, 8192 - c0)
                d_out.inc(sy.dma_start(out=dbg_hbm[:, c0:c0 + w],
                                       in_=dbg_fill[:, 0:w]))
            d_out.wait(sy)
        on("sync", f_fin)

    def cut():
        finish_early()
        while open_cms:
            open_cms.pop().__exit__(None, None, None)
        es.close()
        return nc

    wqkv_cm = nc.sbuf_tensor("wqkv_sb", [128, NC * 3 * D], BF16)
    wqkv_sb = cm_enter(wqkv_cm)
    wout_cm = nc.sbuf_tensor("wout_sb", [128, NC * D], BF16)
    wout_sb = cm_enter(wout_cm)
    wmgA_cm = nc.sbuf_tensor("wmgA", [128, NC * 256], F32)
    wmgA = cm_enter(wmgA_cm)
    wmgB_cm = nc.sbuf_tensor("wmgB", [128, NC * 256], F32)
    wmgB = cm_enter(wmgB_cm)
    garow_cm = nc.sbuf_tensor("garow", [1, 2 * D], F32)
    garow = cm_enter(garow_cm)
    gclamp_cm = nc.sbuf_tensor("gclamp", [1, 2 * D], F32)
    gclamp = cm_enter(gclamp_cm)
    rowbuf_cm = nc.sbuf_tensor("rowbuf", [1, 256], F32)
    rowbuf = cm_enter(rowbuf_cm)
    crow_cm = nc.sbuf_tensor("crow", [1, D], F32)
    crow = cm_enter(crow_cm)

    def f_sy0(sy):
        d_ws.inc(sy.dma_start(out=ident[:], in_=cstf_hbm))
        d_ws.inc(sy.dma_start(out=identb[:], in_=cstb_hbm[0:128, :]))
        d_ws.inc(sy.dma_start(out=maskp[:], in_=cstb_hbm[128:256, :]))
        d_ws.inc(sy.dma_start(out=maskd[:], in_=cstb_hbm[256:384, :]))
        d_ws.inc(sy.dma_start(out=bdiag[:], in_=cstb_hbm[384:512, :]))
        d_ws.inc(sy.dma_start(out=ones1[:], in_=cstb_hbm[512:513, :]))
        d_ws.inc(sy.dma_start(out=negones1[:], in_=cstb_hbm[513:514, :]))
        d_ws.inc(sy.dma_start(out=maskadd[:, 0:128], in_=cstb_hbm[516:644, :]))
        d_ws.inc(sy.dma_start(out=maskadd[:, 128:256], in_=cstb_hbm[644:772, :]))
        d_ws.inc(sy.dma_start(out=onescol[:],
                              in_=cstb_hbm[512:513, :].rearrange("a (p b) -> p (a b)", p=128)))
        d_ws.inc(sy.dma_start(out=crow[:], in_=c_hbm))
        d_ws.inc(sy.dma_start(out=biasrow[:], in_=bias_hbm))
        for t in range(n_tiles):
            d_x.inc(sy.dma_start(
                out=x_sb[:, t * NSUB * D:(t + 1) * NSUB * D],
                in_=x_hbm.rearrange("(s p) d -> p s d", p=128)
                [:, t * NSUB:(t + 1) * NSUB, :]))
    on("sync", f_sy0)

    def f_g0(g):
        g.memset(kmpair[:], 0.0)
        g.memset(dbg_fill[:], 0.0)
        g.memset(ccin_s[:], 0.0)
        g.memset(eps_c[:], EPS)
        g.memset(flv[:], 0.0)
        g.memset(fls[:], 0.0)
        g.memset(qz_sb[:], 0.0)
        li = g.memset(tiny_c[:], 1e-30)
        pg.inc("gp", li)
        d_wg.inc(g.dma_start(
            out=r3(wqkv_sb[:], 3 * D),
            in_=wqkv_hbm.rearrange("(c p) f -> p c f", p=128)))
        d_wg.inc(g.dma_start(
            out=r3(wout_sb[:], D),
            in_=wout_hbm.rearrange("(c p) f -> p c f", p=128)))
    on("gpsimd", f_g0)

    def f_t0(t):
        pg.wait(t, "gp", 1)
        d_ws.wait(t)
        for c in range(NC):
            li = t.transpose(PD[:, c:c + 1], crow[0:1, c * 128:(c + 1) * 128],
                             ident[0:1, 0:1])
        pg.inc("tp", li)
    on("tensor", f_t0)

    def f_s0(s):
        pg.wait(s, "tp")
        s.activation(silu_f[:, 0:NC], PD[:, 0:NC], ACT.Silu)
        s.copy(fls[:], eps_c[:])
        li = s.copy(fls[:], eps_c[:])
        pg.inc("sp", li)
    on("scalar", f_s0)

    def f_v0(v):
        pg.wait(v, "sp")
        v.tensor_copy(silu_b[:], silu_f[:])
        v.tensor_copy(flv[:], eps_c[:])
        v.tensor_copy(flv[:], eps_c[:])
        v.tensor_tensor(silu_lo[:], silu_f[:], silu_b[:], AT.subtract)
        v.tensor_copy(flv[:], eps_c[:])
        li = v.tensor_copy(flv[:], eps_c[:])
        pg.inc("vp", li)
    on("vector", f_v0)

    pieces = ([(cc, False) for cc in range(0, 2048, 256)] +
              [(cc, False) for cc in range(3072, 5120, 256)] +
              [(cc, True) for cc in range(2048, 3072, 256)] +
              [(cc, True) for cc in range(5120, 6144, 256)])
    import os as _os
    KADIS = int(_os.environ.get("KADIS", "9"))
    _np_pieces = int(_os.environ.get("KPIECES", str(len(pieces))))
    pieces = pieces[:_np_pieces]
    buf_last_use = {0: 0, 1: 0}
    for pi, (col0, is32) in enumerate(pieces):
        bi = pi % 2
        buf = (wmgA, wmgB)[bi]
        slot = PB[0:1, bi * 512: bi * 512 + 256]

        def f_syp(sy, buf=buf, bi=bi, col0=col0):
            if buf_last_use[bi]:
                pg.wait(sy, "tp", buf_last_use[bi])
            d_ws.inc(sy.dma_start(
                out=r3(buf[:], 256),
                in_=wmod_hbm.rearrange("(c p) f -> p c f", p=128)[:, :, col0:col0 + 256]))
        on("sync", f_syp)

        def f_vc(v, buf=buf, is32=is32):
            d_ws.wait(v)
            pg.wait(v, "tp")
            li = v.tensor_copy(scr_bf[:, 0:2048], buf[:])
            if is32:
                li = v.tensor_tensor(scr_bf[:, 2048:4096], buf[:],
                                     scr_bf[:, 0:2048], AT.subtract)
            pg.inc("vp", li)
        on("vector", f_vc)

        def f_tm(t, slot=slot, is32=is32, pi=pi, bi=bi):
            pg.wait(t, "vp")
            if pi >= 2:
                pg.wait(t, "sp")
            chains = [(silu_b, 0)]
            if is32:
                chains = [(silu_b, 0), (silu_b, 2048), (silu_lo, 0)]
            n_mm = len(chains) * NC
            k = 0
            for lh, off in chains:
                for c in range(NC):
                    k += 1
                    li = t.matmul(slot, lh[:, c:c + 1],
                                  scr_bf[:, off + c * 256: off + (c + 1) * 256],
                                  start=(k == 1), stop=(k == n_mm))
            buf_last_use[bi] = pg.inc("tp", li)
        on("tensor", f_tm)

        if is32:
            ga_off = (col0 - 2048) if col0 < 3072 else (col0 - 5120 + 1024)

            def f_sev(s, slot=slot, ga_off=ga_off):
                pg.wait(s, "tp")
                li = s.copy(garow[0:1, ga_off:ga_off + 256], slot)
                pg.inc("sp", li)
            on("scalar", f_sev)
        else:
            vec = col0 // D
            d0 = col0 % D

            def f_sev(s, slot=slot):
                pg.wait(s, "tp")
                s.copy(rowbuf[:], slot)
                s.copy(fls[:], eps_c[:])
                li = s.copy(fls[:], eps_c[:])
                pg.inc("sp", li)
            on("scalar", f_sev)

            def f_tr(t):
                pg.wait(t, "sp")
                for i in range(2):
                    li = t.transpose(PD[:, 32 + i:33 + i],
                                     rowbuf[0:1, i * 128:(i + 1) * 128],
                                     ident[0:1, 0:1])
                pg.inc("tp", li)
            on("tensor", f_tr)

            dst, add1 = {0: (sa_fm, False), 1: (ca1_fm, True),
                         3: (sb_fm, False), 4: (cb1_fm, True)}[vec]

            def f_sev2(s, dst=dst, add1=add1, d0=d0):
                pg.wait(s, "tp")
                if add1:
                    li = s.add(dst[:, d0 // 128: d0 // 128 + 2], PD[:, 32:34], 1.0)
                else:
                    li = s.copy(dst[:, d0 // 128: d0 // 128 + 2], PD[:, 32:34])
                pg.inc("sp", li)
            on("scalar", f_sev2)

    if _np_pieces < 24 or int(_os.environ.get("KCUT2", "0")) == 1:
        return cut()

    def f_s1(s):
        pg.wait(s, "sp")
        s.copy(fls[:], eps_c[:])
        s.copy(fls[:], eps_c[:])
        li = s.activation(gclamp[:], garow[:], ACT.Abs)
        pg.inc("sp", li)
    on("scalar", f_s1)

    def f_v1(v):
        pg.wait(v, "sp")
        li = v.tensor_scalar_max(gclamp[:], gclamp[:], 0.25)
        pg.inc("vp", li)
    on("vector", f_v1)

    def f_s1b(s):
        pg.wait(s, "vp")
        li = s.activation(garow[:], garow[:], ACT.Sign)
        pg.inc("sp", li)
    on("scalar", f_s1b)

    def f_v1b(v):
        pg.wait(v, "sp")
        v.tensor_mul(garow[:], garow[:], gclamp[:])
        v.tensor_copy(gbrow[:], garow[0:1, D:2 * D])
        li = v.tensor_copy(scr_bf[0:1, 0:1024], garow[0:1, 0:D])
        pg.inc("vp", li)
    on("vector", f_v1b)

    def f_t1(t):
        pg.wait(t, "vp")
        t.matmul(PA[:, 0:512], ones1[:], scr_bf[0:1, 0:512], start=True, stop=True)
        li = t.matmul(PA[:, 512:1024], ones1[:], scr_bf[0:1, 512:1024],
                      start=True, stop=True)
        pg.inc("tp", li)
    on("tensor", f_t1)

    def f_s1c(s):
        pg.wait(s, "tp")
        li = s.copy(ga_b[:], PA[:, 0:1024])
        pg.inc("sp", li)
    on("scalar", f_s1c)

    if stage <= 2:
        return cut()

    cm_exit(crow_cm)
    cm_exit(rowbuf_cm)
    cm_exit(gclamp_cm)
    cm_exit(garow_cm)
    cm_exit(wmgB_cm)
    cm_exit(wmgA_cm)

    q_cm = nc.sbuf_tensor("q_sb", [128, NC * TILE], BF16)
    q_sb = cm_enter(q_cm)
    k_cm = nc.sbuf_tensor("k_sb", [128, NC * TILE], BF16)
    k_sb = cm_enter(k_cm)
    kp_cm = nc.sbuf_tensor("kp_sb", [128, NC * 128], BF16)
    kp_sb = cm_enter(kp_cm)
    v_cm = nc.sbuf_tensor("v_sb", [128, NSUB * H * 65], BF16)
    v_sb = cm_enter(v_cm)
    vp_cm = nc.sbuf_tensor("vp_sb", [128, H * 65], BF16)
    vp_sb = cm_enter(vp_cm)
    h_cm = nc.sbuf_tensor("h_sb", [128, NC * TILE], BF16)
    h_sb = cm_enter(h_cm)
    pt_cm = nc.sbuf_tensor("pt_sb", [128, H * 256], BF16)
    pt_sb = cm_enter(pt_cm)

    at_view = pt_sb[:, 0:H * 64]
    af_view = pt_sb[:, 2048:2048 + NC * 128]
    scr32 = scr_bf[:].bitcast(F32)
    tr32 = scr32[:, 1024:2048]
    qz = qz_sb[:]


    def f_vones(v):
        d_ws.wait(v)
        li = v.tensor_copy(
            r3(v_sb[:], 65)[:, :, 64:65],
            onescol[:].rearrange("p (a b) -> p a b", b=1)
            .broadcast_to([128, NSUB * H, 1]))
        pg.inc("vp", li)
    on("vector", f_vones)

    def xs_ln(ti, su):
        return x_sb[:, (ti * NSUB + su) * D:(ti * NSUB + su + 1) * D]

    for ti in range(n_tiles):
        sp0 = pg.cnt["sp"]
        vp0 = pg.cnt["vp"]

        def f_sln(s, vp0=vp0, sp0=sp0, ti=ti):
            d_x.wait(s, 16 * (ti + 1))
            pg.wait(s, "tp")
            pg.wait(s, "vp")
            for su in range(NSUB):
                xs = x_sb[:, (ti * NSUB + su) * D:(ti * NSUB + su + 1) * D]
                li = s.activation(scr_bf[:, su * D:(su + 1) * D], xs, ACT.Square,
                                  accum_out=ssq4[:, su:su + 1])
            pg.inc("sp", li)
            pg.wait(s, "vp", vp0 + 1)
            for su in range(NSUB):
                s.activation(lntmp4[:, su:su + 1], var4[:, su:su + 1],
                             ACT.Ln, bias=eps_c[:])
            for su in range(NSUB):
                li = s.activation(aa4[:, su:su + 1], lntmp4[:, su:su + 1],
                                  ACT.Exp, scale=-0.5)
            pg.inc("sp", li)
        on("scalar", f_sln)

        def f_vln(v, ti=ti, sp0=sp0):
            d_x.wait(v, 16 * (ti + 1))
            for su in range(NSUB):
                xs = x_sb[:, (ti * NSUB + su) * D:(ti * NSUB + su + 1) * D]
                v.reduce_sum(sxa4[:, su:su + 1], xs, axis=AX.X)
            for su in range(NSUB):
                v.tensor_scalar_mul(mean4[:, su:su + 1], sxa4[:, su:su + 1], INV_D)
            for su in range(NSUB):
                v.tensor_tensor(msq4[:, su:su + 1], mean4[:, su:su + 1],
                                mean4[:, su:su + 1], AT.mult)
            pg.wait(v, "sp", sp0 + 1)
            for su in range(NSUB):
                li = v.tensor_scalar(var4[:, su:su + 1], ssq4[:, su:su + 1],
                                     INV_D, msq4[:, su:su + 1],
                                     AT.mult, AT.subtract)
            pg.inc("vp", li)
            pg.wait(v, "sp", sp0 + 2)
            for su in range(NSUB):
                v.tensor_scalar(bb4[:, su:su + 1], mean4[:, su:su + 1],
                                aa4[:, su:su + 1], -1.0, AT.mult, AT.mult)
            for su in range(NSUB):
                li = v.tensor_scalar(scr_bf[:, su * D:(su + 1) * D], xs_ln(ti, su),
                                     aa4[:, su:su + 1], bb4[:, su:su + 1],
                                     AT.mult, AT.add)
            pg.inc("vp", li)
        on("vector", f_vln)

        for rnd in range(4):
            reg = (PA, PB)[rnd % 2]

            def f_tt(t, rnd=rnd, reg=reg):
                pg.wait(t, "vp")
                pg.wait(t, "sp")
                regb = reg[:].bitcast(BF16)
                for cl, c in enumerate((2 * rnd, 2 * rnd + 1)):
                    for su in range(NSUB):
                        li = t.transpose(
                            regb[:, cl * 512 + su * 128: cl * 512 + (su + 1) * 128],
                            scr_bf[:, su * D + c * 128: su * D + (c + 1) * 128],
                            identb[:])
                pg.inc("tp", li)
            on("tensor", f_tt)

            def f_st(s, rnd=rnd, reg=reg):
                pg.wait(s, "tp")
                regb = reg[:].bitcast(BF16)
                for cl, c in enumerate((2 * rnd, 2 * rnd + 1)):
                    li = s.activation(h_sb[:, c * 512:(c + 1) * 512],
                                      regb[:, cl * 512:(cl + 1) * 512], ACT.Identity,
                                      bias=sa_fm[:, c:c + 1], scale=ca1_fm[:, c:c + 1])
                pg.inc("sp", li)
            on("scalar", f_st)

        for part, f0, dst, scale in (("q", 0, q_sb, 0.125), ("k", D, k_sb, None)):
            for fc in range(NC):
                half = (PA, PB)[fc % 2]

                def f_tqk(t, fc=fc, f0=f0, half=half, part=part, ti=ti):
                    if part == "q" and fc == 0:
                        pg.wait(t, "sp")
                        if ti == 0:
                            d_wg.wait(t)
                    pg.wait(t, "vp")
                    for dcc in range(NC):
                        li = t.matmul(half[:, 0:512],
                                      wqkv_sb[:, dcc * 3 * D + f0 + fc * 128:
                                              dcc * 3 * D + f0 + (fc + 1) * 128],
                                      h_sb[:, dcc * 512:(dcc + 1) * 512],
                                      start=(dcc == 0), stop=(dcc == NC - 1))
                    pg.inc("tp", li)
                on("tensor", f_tqk)

                def f_vqk(v, fc=fc, dst=dst, half=half, scale=scale):
                    pg.wait(v, "tp")
                    if scale is not None:
                        li = v.tensor_scalar(dst[:, fc * 512:(fc + 1) * 512],
                                             half[:, 0:512], scale, None, AT.mult)
                    else:
                        li = v.tensor_copy(dst[:, fc * 512:(fc + 1) * 512],
                                           half[:, 0:512])
                    pg.inc("vp", li)
                on("vector", f_vqk)

        for su in range(NSUB):
            half = (PA, PB)[su % 2]

            def f_tv(t, su=su, half=half):
                pg.wait(t, "vp")
                for dcc in range(NC):
                    lh = h_sb[:, dcc * 512 + su * 128: dcc * 512 + (su + 1) * 128]
                    t.matmul(half[:, 0:512], lh,
                             wqkv_sb[:, dcc * 3 * D + 2 * D: dcc * 3 * D + 2 * D + 512],
                             start=(dcc == 0), stop=(dcc == NC - 1))
                    li = t.matmul(half[:, 512:1024], lh,
                                  wqkv_sb[:, dcc * 3 * D + 2 * D + 512: dcc * 3 * D + 3 * D],
                                  start=(dcc == 0), stop=(dcc == NC - 1))
                pg.inc("tp", li)
            on("tensor", f_tv)

            def f_vv(v, su=su, half=half):
                pg.wait(v, "tp")
                li = v.tensor_copy(
                    r3(v_sb[:, su * H * 65:(su + 1) * H * 65], 65)[:, :, 0:64],
                    r3(half[:, 0:1024], 64))
                pg.inc("vp", li)
            on("vector", f_vv)

        def f_sk(s):
            pg.wait(s, "vp")
            s.activation(scr_bf[:, 0:2048], k_sb[:, 0:2048], ACT.Square)
            li = s.activation(scr_bf[:, 2048:4096], k_sb[:, 2048:4096], ACT.Square)
            pg.inc("sp", li)
        on("scalar", f_sk)

        def f_tk(t):
            pg.wait(t, "sp")
            for c in range(NC):
                li = t.matmul(PD[0:16, 0:512], bdiag[:, c * 16:(c + 1) * 16],
                              scr_bf[:, c * 512:(c + 1) * 512],
                              start=(c == 0), stop=(c == NC - 1))
            pg.inc("tp", li)
        on("tensor", f_tk)

        def f_vk(v, ti=ti):
            pg.wait(v, "tp")
            if ti == 0:
                pg.wait(v, "gp", 1)
            v.reduce_max(kmpair[:, 1:2], PD[0:16, 0:512], axis=AX.X)
            v.tensor_copy(flv[:], eps_c[:])
            v.tensor_copy(flv[:], eps_c[:])
            v.reduce_max(km_use[:], kmpair[:, 0:2], axis=AX.X)
            v.tensor_copy(kmpair[:, 0:1], kmpair[:, 1:2])
            li = v.tensor_copy(flv[:], eps_c[:])
            pg.inc("vp", li)
        on("vector", f_vk)

        def f_tk2(t):
            pg.wait(t, "vp")
            li = t.transpose(PD[0:1, 512:528], km_use[:], ident[0:16, 0:16])
            pg.inc("tp", li)
        on("tensor", f_tk2)

        def f_sk2(s):
            pg.wait(s, "tp")
            s.activation(kmrow_bf[:], PD[0:1, 512:528], ACT.Identity)
            s.copy(fls[:], eps_c[:])
            li = s.copy(fls[:], eps_c[:])
            pg.inc("sp", li)
        on("scalar", f_sk2)

        def f_tk4(t):
            pg.wait(t, "sp")
            li = t.matmul(PD[:, 528:544], ones1[:], kmrow_bf[:],
                          start=True, stop=True)
            pg.inc("tp", li)
        on("tensor", f_tk4)

        def f_vk3(v):
            pg.wait(v, "tp")
            v.tensor_copy(kmb16[:], PD[:, 528:544])
            v.tensor_copy(flv[:], eps_c[:])
            v.tensor_copy(km128[0:64, :], r3(kmb16[0:64, :], 2)[:, :, 0:1])
            v.tensor_copy(km128[64:128, :], r3(kmb16[64:128, :], 2)[:, :, 1:2])
            v.tensor_copy(flv[:], eps_c[:])
            li = v.tensor_copy(flv[:], eps_c[:])
            pg.inc("vp", li)
        on("vector", f_vk3)

        def f_sq4(s):
            pg.wait(s, "tp")
            li = s.activation(scr_bf[:, 0:4096], q_sb[:, 0:4096], ACT.Square)
            pg.inc("sp", li)
        on("scalar", f_sq4)

        def f_tb4(t):
            pg.wait(t, "vp")
            pg.wait(t, "sp")
            for su in range(NSUB):
                for c in range(NC):
                    li = t.matmul(PD[0:1, 512 + su * 128: 640 + su * 128],
                                  km128[:, c:c + 1],
                                  scr_bf[:, c * 512 + su * 128:
                                         c * 512 + (su + 1) * 128],
                                  start=(c == 0), stop=(c == NC - 1))
            pg.inc("tp", li)
        on("tensor", f_tb4)

        def f_sb24(s):
            pg.wait(s, "tp")
            s.activation(row512[:], PD[0:1, 512:1024], ACT.Ln, bias=tiny_c[:])
            s.copy(fls[:], eps_c[:])
            s.copy(fls[:], eps_c[:])
            s.activation(brow4[:], row512[:], ACT.Exp, scale=0.5)
            s.copy(fls[:], eps_c[:])
            li = s.copy(fls[:], eps_c[:])
            pg.inc("sp", li)
        on("scalar", f_sb24)

        for su in range(NSUB):
            if KADIS < 1:
                break
            first_global = (ti == 0 and su == 0)

            def f_vqz(v, su=su, ti=ti):
                pg.wait(v, "tp")
                if ti == 0 and su == 0:
                    pg.wait(v, "gp", 1)
                qz4 = qz.rearrange("p (a b c) -> p a b c", b=2, c=128)
                v.tensor_copy(qz4[0:64, :, 0, :],
                              r3(q_sb[0:64, :], 512)[:, :, su * 128:(su + 1) * 128])
                li = v.tensor_copy(qz4[64:128, :, 1, :],
                                   r3(q_sb[64:128, :], 512)[:, :, su * 128:(su + 1) * 128])
                pg.inc("vp", li)
            on("vector", f_vqz)

            if KADIS < 2:
                continue
            for g4 in range(4):
                reg = (PB, PC)[g4 % 2]

                def f_ts(t, g4=g4, reg=reg, su=su, first_global=first_global):
                    pg.wait(t, "sp")
                    pg.wait(t, "vp")
                    bspan = brow4[0:1, su * 128:(su + 1) * 128] \
                        .rearrange("p (a b c) -> p a b c", b=1, c=128) \
                        .broadcast_to([1, 2, 2, 128])
                    qz4 = qz.rearrange("p (a b c) -> p a b c", b=2, c=128)
                    madd = maskadd[:].rearrange("p (a b t) -> p a b t", a=1, b=2) \
                        .broadcast_to([128, 2, 2, 128])
                    for bank in range(2):
                        out4 = r3(reg[:, bank * 512:(bank + 1) * 512], 128) \
                            .rearrange("p (a b) c -> p a b c", b=2)
                        t.matmul(out4, identb[:], madd, start=True, stop=False)
                        t.matmul(out4, negones1[:], bspan, start=False, stop=False)
                        for hh in range(2):
                            h = g4 * 4 + bank * 2 + hh
                            par = h % 2
                            pr = h // 2
                            qrhs = qz4[:, pr, par, :]
                            col0 = bank * 512 + hh * 256
                            if not first_global:
                                if su == 0:
                                    klh = kp_sb[:, pr * 128:(pr + 1) * 128]
                                else:
                                    klh = k_sb[:, pr * 512 + (su - 1) * 128:
                                               pr * 512 + su * 128]
                                t.matmul(reg[:, col0:col0 + 128], klh, qrhs,
                                         start=False, stop=False)
                            li = t.matmul(reg[:, col0 + 128:col0 + 256],
                                          k_sb[:, pr * 512 + su * 128:
                                               pr * 512 + (su + 1) * 128],
                                          qrhs, start=False, stop=(hh == 1))
                    pg.inc("tp", li)
                on("tensor", f_ts)

                def f_se(s, g4=g4, reg=reg):
                    pg.wait(s, "tp")
                    li = s.activation(pt_sb[:, g4 * 1024:(g4 + 1) * 1024],
                                      reg[:], ACT.Exp)
                    pg.inc("sp", li)
                on("scalar", f_se)

            if KADIS < 3:
                continue

            def f_tpv(t, su=su, first_global=first_global):
                pg.wait(t, "sp")
                pg.wait(t, "vp")
                for rr in range(2):
                    reg = (PD, PC)[rr]
                    for hh in range(8):
                        h = rr * 8 + hh
                        outp = reg[:, hh * 128: hh * 128 + 65]
                        started = False
                        if not first_global:
                            if su == 0:
                                vrhs = vp_sb[:, h * 65:(h + 1) * 65]
                            else:
                                vrhs = v_sb[:, ((su - 1) * H + h) * 65:
                                            ((su - 1) * H + h + 1) * 65]
                            t.matmul(outp, pt_sb[:, h * 256: h * 256 + 128], vrhs,
                                     start=True, stop=False)
                            started = True
                        li = t.matmul(outp, pt_sb[:, h * 256 + 128: h * 256 + 256],
                                      v_sb[:, (su * H + h) * 65:(su * H + h + 1) * 65],
                                      start=not started, stop=True)
                pg.inc("tp", li)
            on("tensor", f_tpv)

            def f_vr(v):
                pg.wait(v, "tp")
                pd3 = r3(PD[:], 128)
                pc3 = r3(PC[:], 128)
                v.tensor_scalar_add(rtmp[:, 0:8], pd3[:, 0:8, 64:65], 1e-30)
                v.tensor_scalar_add(rtmp[:, 8:16], pc3[:, 0:8, 64:65], 1e-30)
                v.reciprocal(rinv[:, 0:8], rtmp[:, 0:8])
                v.reciprocal(rinv[:, 8:16], rtmp[:, 8:16])
                for rr, p3 in ((0, pd3), (1, pc3)):
                    li = v.tensor_tensor(
                        r3(at_view[:, rr * 512:(rr + 1) * 512], 64),
                        p3[:, 0:8, 0:64],
                        rinv[:, rr * 8:(rr + 1) * 8]
                        .rearrange("p (a b) -> p a b", b=1).broadcast_to([128, 8, 64]),
                        AT.mult)
                pg.inc("vp", li)
            on("vector", f_vr)

            if KADIS < 4:
                continue

            def f_ta(t):
                pg.wait(t, "vp")
                pbb = PB[:].bitcast(BF16)
                for c in range(NC):
                    li = t.transpose(pbb[:, c * 128:(c + 1) * 128],
                                     at_view[:, c * 128:(c + 1) * 128], identb[:])
                pg.inc("tp", li)
            on("tensor", f_ta)

            def f_va(v):
                pg.wait(v, "tp")
                li = v.tensor_copy(af_view[:], PB[:].bitcast(BF16)[:, 0:1024])
                pg.inc("vp", li)
            on("vector", f_va)

            KDSU = int(_os.environ.get("KDSU", "-1"))
            if ti == 0 and su == KDSU:
                def f_dbg(g, su=su):
                    pg.wait(g, "vp")
                    pg.wait(g, "sp")
                    d_cc.inc(g.dma_start(out=dbg_hbm[:, 0:4096], in_=pt_sb[:]))
                    d_cc.inc(g.dma_start(out=dbg_hbm[:, 4096:4112], in_=rinv[:]))
                    d_cc.inc(g.dma_start(out=dbg_hbm[0:1, 4240:4368],
                                          in_=brow4[0:1, su * 128:(su + 1) * 128]))
                    d_cc.inc(g.dma_start(out=dbg_hbm[:, 4368:4376], in_=km128[:]))
                    d_cc.inc(g.dma_start(out=dbg_hbm[:, 4376:4392], in_=kmb16[:]))
                    d_cc.inc(g.dma_start(out=dbg_hbm[0:16, 4392:4393], in_=km_use[:], allow_slow_non_contiguous=True))
                    d_cc.inc(g.dma_start(out=dbg_hbm[0:16, 4393:4395], in_=kmpair[:], allow_slow_non_contiguous=True))
                    d_cc.inc(g.dma_start(out=dbg_hbm[:, 4416:5440],
                                          in_=scr_bf[:, 0:1024]))
                    d_cc.inc(g.dma_start(out=dbg_hbm[:, 5440:6464],
                                          in_=q_sb[:, 0:1024]))
                    d_cc.inc(g.dma_start(out=dbg_hbm[:, 6464:7488],
                                          in_=k_sb[:, 0:1024]))
                    d_cc.inc(g.dma_start(out=dbg_hbm[0:1, 7488:7489], in_=tiny_c[:]))
                    d_cc.inc(g.dma_start(out=dbg_hbm[:, 7489:7497],
                                          in_=silu_f[:, 0:8]))
                    d_cc.inc(g.dma_start(out=dbg_hbm[:, 7497:7505],
                                          in_=ca1_fm[:]))
                    d_cc.inc(g.dma_start(out=dbg_hbm[:, 7505:7513],
                                          in_=km128[:].bitcast(BF16)[:, 0:8]))
                    d_cc.inc(g.dma_start(out=dbg_hbm[:, 7520:8032],
                                          in_=h_sb[:, 0:512]))
                    d_cc.inc(g.dma_start(out=dbg_hbm[:, 8032:8096],
                                          in_=x_sb[:, 0:64]))
                    d_cc.inc(g.dma_start(out=dbg_hbm[:, 8096:8160],
                                          in_=wqkv_sb[:, 0:64]))
                on("gpsimd", f_dbg)

            if KADIS < 5:
                continue

            def f_tp(t):
                pg.wait(t, "vp")
                for dcc in range(NC):
                    lh = af_view[:, dcc * 128:(dcc + 1) * 128]
                    t.matmul(PA[:, 0:512], lh, wout_sb[:, dcc * D: dcc * D + 512],
                             start=(dcc == 0), stop=(dcc == NC - 1))
                    li = t.matmul(PA[:, 512:1024], lh,
                                  wout_sb[:, dcc * D + 512:(dcc + 1) * D],
                                  start=(dcc == 0), stop=(dcc == NC - 1))
                pg.inc("tp", li)
            on("tensor", f_tp)

            if KADIS < 6:
                continue

            def f_vres(v, ti=ti, su=su):
                pg.wait(v, "tp")
                xs = x_sb[:, (ti * NSUB + su) * D:(ti * NSUB + su + 1) * D]
                v.tensor_tensor(tr32, PA[:, 0:1024], ga_b[:], AT.mult)
                li = v.tensor_add(xs, xs, tr32)
                pg.inc("vp", li)
            on("vector", f_vres)

            if int(_os.environ.get("KONESU", "0")) and ti == 0 and su == 0:
                return cut()

        def f_vring(v):
            v.tensor_copy(r3(kp_sb[:], 128), r3(k_sb[:], 512)[:, :, 384:512])
            li = v.tensor_copy(vp_sb[:], v_sb[:, 3 * H * 65:4 * H * 65])
            pg.inc("vp", li)
        on("vector", f_vring)

    if stage <= 4:
        return cut()

    cm_exit(pt_cm)
    cm_exit(h_cm)
    cm_exit(vp_cm)
    cm_exit(v_cm)
    cm_exit(kp_cm)
    cm_exit(k_cm)
    cm_exit(q_cm)
    cm_exit(wout_cm)
    cm_exit(wqkv_cm)
    h2_cm = nc.sbuf_tensor("h2_sb", [128, NC * S], BF16)
    h2_sb = cm_enter(h2_cm)

    a2_tp = pg.cnt["tp"]
    a2_vp = pg.cnt["vp"]
    a2_sp = pg.cnt["sp"]

    for ti in range(n_tiles):
        sp0 = pg.cnt["sp"]
        vp0 = pg.cnt["vp"]

        def f_sln2(s, vp0=vp0, sp0=sp0, ti=ti):
            pg.wait(s, "tp")
            if ti == 0:
                pg.wait(s, "vp")
            for su in range(NSUB):
                xs = x_sb[:, (ti * NSUB + su) * D:(ti * NSUB + su + 1) * D]
                li = s.activation(scr_bf[:, su * D:(su + 1) * D], xs, ACT.Square,
                                  accum_out=ssq4[:, su:su + 1])
            pg.inc("sp", li)
            pg.wait(s, "vp", vp0 + 1)
            for su in range(NSUB):
                s.activation(lntmp4[:, su:su + 1], var4[:, su:su + 1],
                             ACT.Ln, bias=eps_c[:])
            for su in range(NSUB):
                li = s.activation(aa4[:, su:su + 1], lntmp4[:, su:su + 1],
                                  ACT.Exp, scale=-0.5)
            pg.inc("sp", li)
        on("scalar", f_sln2)

        def f_vln2(v, ti=ti, sp0=sp0):
            for su in range(NSUB):
                xs = x_sb[:, (ti * NSUB + su) * D:(ti * NSUB + su + 1) * D]
                v.reduce_sum(sxa4[:, su:su + 1], xs, axis=AX.X)
            for su in range(NSUB):
                v.tensor_scalar_mul(mean4[:, su:su + 1], sxa4[:, su:su + 1], INV_D)
            for su in range(NSUB):
                v.tensor_tensor(msq4[:, su:su + 1], mean4[:, su:su + 1],
                                mean4[:, su:su + 1], AT.mult)
            pg.wait(v, "sp", sp0 + 1)
            for su in range(NSUB):
                li = v.tensor_scalar(var4[:, su:su + 1], ssq4[:, su:su + 1],
                                     INV_D, msq4[:, su:su + 1],
                                     AT.mult, AT.subtract)
            pg.inc("vp", li)
            pg.wait(v, "sp", sp0 + 2)
            for su in range(NSUB):
                v.tensor_scalar(bb4[:, su:su + 1], mean4[:, su:su + 1],
                                aa4[:, su:su + 1], -1.0, AT.mult, AT.mult)
            for su in range(NSUB):
                li = v.tensor_scalar(scr_bf[:, su * D:(su + 1) * D], xs_ln(ti, su),
                                     aa4[:, su:su + 1], bb4[:, su:su + 1],
                                     AT.mult, AT.add)
            pg.inc("vp", li)
        on("vector", f_vln2)

        for rnd in range(4):
            reg = (PA, PB)[rnd % 2]

            def f_tt2(t, rnd=rnd, reg=reg):
                pg.wait(t, "vp")
                pg.wait(t, "sp")
                regb = reg[:].bitcast(BF16)
                for cl, c in enumerate((2 * rnd, 2 * rnd + 1)):
                    for su in range(NSUB):
                        li = t.transpose(
                            regb[:, cl * 512 + su * 128: cl * 512 + (su + 1) * 128],
                            scr_bf[:, su * D + c * 128: su * D + (c + 1) * 128],
                            identb[:])
                pg.inc("tp", li)
            on("tensor", f_tt2)

            def f_st2(s, rnd=rnd, reg=reg, ti=ti):
                pg.wait(s, "tp")
                regb = reg[:].bitcast(BF16)
                for cl, c in enumerate((2 * rnd, 2 * rnd + 1)):
                    li = s.activation(h2_sb[:, c * S + ti * 512: c * S + (ti + 1) * 512],
                                      regb[:, cl * 512:(cl + 1) * 512], ACT.Identity,
                                      bias=sb_fm[:, c:c + 1], scale=cb1_fm[:, c:c + 1])
                pg.inc("sp", li)
            on("scalar", f_st2)

        def f_vam(v, ti=ti):
            pg.wait(v, "sp")
            li = v.reduce_max(amax16[:, ti:ti + 1],
                              h2_sb[:].rearrange("p (c t) -> p c t", t=S)
                              [:, :, ti * 512:(ti + 1) * 512],
                              axis=AX.XY, apply_absolute_value=True)
            pg.inc("vp", li)
        on("vector", f_vam)

    wq8_cm = nc.sbuf_tensor("wq8_sb", [128, NC * D], F8)
    wq8_sb = cm_enter(wq8_cm)
    wst_cm = nc.sbuf_tensor("wst", [128, NC * D], F32)
    wst = cm_enter(wst_cm)

    def f_sy2(sy):
        pg.wait(sy, "tp", a2_tp)
        pg.wait(sy, "vp", a2_vp)
        pg.wait(sy, "sp", a2_sp)
        d_ws.inc(sy.dma_start(
            out=r3(wst[:], D),
            in_=wffn_hbm.rearrange("(c p) f -> p c f", p=128)))
    on("sync", f_sy2)

    def f_v2(v):
        d_ws.wait(v)
        v.reduce_max(kmb16[:, 0:1], r3(wst[:], D), axis=AX.XY,
                     apply_absolute_value=True)
        v.tensor_copy(flv[:], eps_c[:])
        li = v.tensor_copy(flv[:], eps_c[:])
        pg.inc("vp", li)
    on("vector", f_v2)

    def f_t2(t):
        pg.wait(t, "vp")
        li = t.transpose(PD[0:1, 48:176], kmb16[:, 0:1], ident[:])
        pg.inc("tp", li)
    on("tensor", f_t2)

    def f_s2(s):
        pg.wait(s, "tp")
        s.copy(row128[:], PD[0:1, 48:176])
        s.copy(fls[:], eps_c[:])
        li = s.copy(fls[:], eps_c[:])
        pg.inc("sp", li)
    on("scalar", f_s2)

    def f_v2b(v):
        pg.wait(v, "sp")
        v.reduce_max(amaxw_s[:], row128[:], axis=AX.X)
        v.tensor_copy(flv[:], eps_c[:])
        v.tensor_copy(flv[:], eps_c[:])
        v.tensor_scalar_max(amaxw_s[:], amaxw_s[:], 1e-12)
        v.tensor_copy(flv[:], eps_c[:])
        v.tensor_copy(flv[:], eps_c[:])
        v.reciprocal(sc1[:], amaxw_s[:])
        v.tensor_copy(flv[:], eps_c[:])
        v.tensor_copy(flv[:], eps_c[:])
        v.tensor_scalar_mul(sc1[:], sc1[:], 224.0)
        v.tensor_copy(flv[:], eps_c[:])
        v.tensor_copy(flv[:], eps_c[:])
        v.tensor_copy(rowb[:], sc1[0:1, 0:1].rearrange("p (a b) -> p a b", b=1)
                      .broadcast_to([1, 128, 1]))
        v.tensor_copy(flv[:], eps_c[:])
        li = v.tensor_copy(flv[:], eps_c[:])
        pg.inc("vp", li)
    on("vector", f_v2b)

    def f_t2b(t):
        pg.wait(t, "vp")
        li = t.transpose(PD[:, 176:177], rowb[0:1, :], ident[0:1, 0:1])
        pg.inc("tp", li)
    on("tensor", f_t2b)

    def f_s2b(s):
        pg.wait(s, "tp")
        s.copy(swh128[:], PD[:, 176:177])
        s.copy(fls[:], eps_c[:])
        li = s.copy(fls[:], eps_c[:])
        pg.inc("sp", li)
    on("scalar", f_s2b)

    for dc in range(NC):
        def f_tq(t, dc=dc):
            pg.wait(t, "sp")
            pg.wait(t, "vp")
            for oc in range(NC):
                li = t.transpose(PA[:, oc * 128:(oc + 1) * 128],
                                 wst[:, oc * D + dc * 128: oc * D + (dc + 1) * 128],
                                 ident[:])
            pg.inc("tp", li)
        on("tensor", f_tq)

        def f_vq(v, dc=dc):
            pg.wait(v, "tp")
            li = v.tensor_scalar(wq8_sb[:, dc * D:(dc + 1) * D], PA[:, 0:1024],
                                 swh128[:], None, AT.mult)
            pg.inc("vp", li)
        on("vector", f_vq)

    cm_exit(wst_cm)

    if stage <= 3:
        return cut()


    def f_vamf(v):
        v.tensor_copy(flv[:], eps_c[:])
        v.tensor_copy(flv[:], eps_c[:])
        v.reduce_max(amax_run[:], amax16[:], axis=AX.X)
        v.tensor_copy(flv[:], eps_c[:])
        li = v.tensor_copy(flv[:], eps_c[:])
        pg.inc("vp", li)
    on("vector", f_vamf)

    def f_tam(t):
        pg.wait(t, "vp")
        li = t.transpose(PD[0:1, 768:896], amax_run[:], ident[:])
        pg.inc("tp", li)
    on("tensor", f_tam)

    def f_sam(s):
        pg.wait(s, "tp")
        s.copy(row128[:], PD[0:1, 768:896])
        s.copy(fls[:], eps_c[:])
        li = s.copy(fls[:], eps_c[:])
        pg.inc("sp", li)
    on("scalar", f_sam)

    def f_vam2(v):
        pg.wait(v, "sp")
        v.reduce_max(sc1[:], row128[:], axis=AX.X)
        v.tensor_copy(flv[:], eps_c[:])
        v.tensor_copy(flv[:], eps_c[:])
        v.tensor_scalar_max(sc1[:], sc1[:], 1e-12)
        v.tensor_copy(flv[:], eps_c[:])
        v.tensor_copy(flv[:], eps_c[:])
        v.tensor_copy(ccin_s[0:1, 0:1], sc1[:])
        v.tensor_copy(flv[:], eps_c[:])
        li = v.tensor_copy(flv[:], eps_c[:])
        pg.inc("vp", li)
    on("vector", f_vam2)

    def f_sycc(sy):
        pg.wait(sy, "vp")
        d_cc.inc(sy.dma_start(out=cc_in.ap(), in_=ccin_s[:]))
    on("sync", f_sycc)

    def f_gcc(g):
        d_cc.wait(g)
        g.collective_compute(
            "AllReduce", AT.max, replica_groups=[list(range(n_cores))],
            ins=[cc_in.ap().opt()], outs=[cc_out.ap().opt()]).then_inc(cc_sem, 1)
    on("gpsimd", f_gcc)

    def f_sycc2(sy):
        sy.wait_ge(cc_sem, 1)
        d_cc.inc(sy.dma_start(out=ccout_s[:], in_=cc_out.ap()))
    on("sync", f_sycc2)

    if stage <= 5:
        return cut()

    g1_cm = nc.sbuf_tensor("g1_b", [128, D], BF16)
    g1_b = cm_enter(g1_cm)
    xq_cm = nc.sbuf_tensor("xq_sb", [128, NC * TILE], F8)
    xq_sb = cm_enter(xq_cm)
    gbc_cm = nc.sbuf_tensor("gbc_row", [1, D], BF16)
    gbc_row = cm_enter(gbc_cm)

    def f_vg(v):
        d_cc.wait(v)
        v.reciprocal(sc2[:], ccout_s[0:1, 0:1])
        v.tensor_copy(flv[:], eps_c[:])
        v.tensor_copy(flv[:], eps_c[:])
        v.tensor_scalar_mul(sc2[:], sc2[:], 224.0)
        v.tensor_copy(flv[:], eps_c[:])
        v.tensor_copy(flv[:], eps_c[:])
        v.tensor_copy(rowb[:], sc2[0:1, 0:1].rearrange("p (a b) -> p a b", b=1)
                      .broadcast_to([1, 128, 1]))
        v.tensor_mul(sc2[:], ccout_s[0:1, 0:1], amaxw_s[:])
        v.tensor_copy(flv[:], eps_c[:])
        v.tensor_copy(flv[:], eps_c[:])
        v.tensor_scalar_mul(sc2[:], sc2[:], 1.0 / 50176.0)
        v.tensor_copy(flv[:], eps_c[:])
        v.tensor_copy(flv[:], eps_c[:])
        v.reciprocal(sc3[:], sc2[:])
        v.tensor_copy(flv[:], eps_c[:])
        v.tensor_copy(flv[:], eps_c[:])
        v.tensor_scalar(biasq[:], biasrow[:], sc3[:], None, AT.mult)
        li = v.tensor_scalar(gbc_row[:], gbrow[:], sc2[:], None, AT.mult)
        pg.inc("vp", li)
    on("vector", f_vg)

    def f_tg(t):
        pg.wait(t, "vp")
        t.transpose(PD[:, 896:897], rowb[0:1, :], ident[0:1, 0:1])
        t.matmul(PA[:, 0:512], ones1[:], gbc_row[0:1, 0:512], start=True, stop=True)
        li = t.matmul(PA[:, 512:1024], ones1[:], gbc_row[0:1, 512:1024],
                      start=True, stop=True)
        pg.inc("tp", li)
    on("tensor", f_tg)

    def f_sg(s):
        pg.wait(s, "tp")
        s.copy(sxh128[:], PD[:, 896:897])
        li = s.copy(g1_b[:], PA[:, 0:1024])
        pg.inc("sp", li)
    on("scalar", f_sg)

    for ti in range(n_tiles):
        def f_vxq(v, ti=ti):
            pg.wait(v, "vp")
            pg.wait(v, "sp")
            li = v.tensor_scalar(
                r3(xq_sb[:], 512),
                h2_sb[:].rearrange("p (c t) -> p c t", t=S)
                [:, :, ti * 512:(ti + 1) * 512],
                sxh128[:], None, AT.mult)
            pg.inc("vp", li)
        on("vector", f_vxq)

        vpB = pg.cnt["vp"]
        for su in range(NSUB):
            regf = (PA, PB)[su % 2]

            def f_tf(t, su=su, regf=regf, vpB=vpB):
                pg.wait(t, "vp", vpB + max(0, su - 1))
                t.matmul(regf[:, 0:512], ones1[:], biasq[0:1, 0:512],
                         start=True, stop=False)
                t.matmul(regf[:, 512:1024], ones1[:], biasq[0:1, 512:1024],
                         start=True, stop=False)
                for dcc in range(NC):
                    lh = xq_sb[:, dcc * 512 + su * 128: dcc * 512 + (su + 1) * 128]
                    t.matmul(regf[:, 0:512], lh, wq8_sb[:, dcc * D: dcc * D + 512],
                             start=False, stop=(dcc == NC - 1))
                    li = t.matmul(regf[:, 512:1024], lh,
                                  wq8_sb[:, dcc * D + 512:(dcc + 1) * D],
                                  start=False, stop=(dcc == NC - 1))
                pg.inc("tp", li)
            on("tensor", f_tf)

            def f_vf(v, ti=ti, su=su, regf=regf):
                pg.wait(v, "tp")
                xs = x_sb[:, (ti * NSUB + su) * D:(ti * NSUB + su + 1) * D]
                v.tensor_tensor(tr32, regf[:, 0:1024], g1_b[:], AT.mult)
                li = v.tensor_add(xs, xs, tr32)
                pg.inc("vp", li)
            on("vector", f_vf)

        def f_syo(sy, ti=ti):
            pg.wait(sy, "vp")
            d_out.inc(sy.dma_start(
                out=out_hbm.rearrange("(s p) d -> p s d", p=128)
                [:, ti * NSUB:(ti + 1) * NSUB, :],
                in_=x_sb[:, ti * NSUB * D:(ti + 1) * NSUB * D]))
        on("sync", f_syo)

    def f_syend(sy):
        for c0 in range(0, 8192, 64):
            d_out.inc(sy.dma_start(out=dbg_hbm[:, c0:c0 + 64], in_=dbg_fill[:])) \
                if False else None
        d_out.wait(sy)
    on("sync", f_syend)

    cm_exit(gbc_cm)
    cm_exit(xq_cm)
    cm_exit(g1_cm)
    cm_exit(wq8_cm)
    cm_exit(h2_cm)
    es.close()
    return nc


def _make_consts():
    import ml_dtypes
    cst_f = np.eye(128, dtype=np.float32)
    t = np.arange(128)
    p = np.arange(128)[:, None]
    rows = [np.eye(128, dtype=np.float32),
            (t[None, :] < p).astype(np.float32),
            (t[None, :] >= p).astype(np.float32)]
    bd = np.zeros((128, 128), dtype=np.float32)
    for c in range(NC):
        bd[0:64, c * 16 + 2 * c] = 1.0
        bd[64:128, c * 16 + 2 * c + 1] = 1.0
    rows.append(bd)
    ones = np.zeros((4, 128), dtype=np.float32)
    ones[0, :] = 1.0
    ones[1, :] = -1.0
    rows.append(ones)
    cst_b = np.concatenate(rows, axis=0)[0:516]
    mka = (1.0 - rows[1]) * -30000.0
    mkd = (1.0 - rows[2]) * -30000.0
    cst_b = np.concatenate([cst_b, mka, mkd], axis=0)
    return cst_f, np.ascontiguousarray(cst_b.astype(ml_dtypes.bfloat16))


_NC_CACHE = {}


def kernel(L0_attn_x, L0_attn_w_qkv, L0_attn_w_out, L0_ffn_weight, L0_ffn_bias,
           L0_c, L0_wmod):
    import os as _os
    stage = int(_os.environ.get("KSTAGE", "99"))
    n_tiles = L0_attn_x.shape[1] // TILE
    n_cores = L0_attn_x.shape[0]
    key = (n_tiles, n_cores, stage)
    if key not in _NC_CACHE:
        _NC_CACHE[key] = build_nc(n_tiles, n_cores, stage)
    nc = _NC_CACHE[key]

    x = np.ascontiguousarray(np.asarray(L0_attn_x, dtype=np.float32))
    wqkv = np.ascontiguousarray(np.asarray(L0_attn_w_qkv, dtype=np.float32))
    wout = np.ascontiguousarray(np.asarray(L0_attn_w_out, dtype=np.float32))
    wffn = np.ascontiguousarray(np.asarray(L0_ffn_weight, dtype=np.float32))
    fbias = np.ascontiguousarray(
        np.asarray(L0_ffn_bias, dtype=np.float32).reshape(1, D))
    c_all = np.ascontiguousarray(np.asarray(L0_c, dtype=np.float32))
    wmod = np.ascontiguousarray(np.asarray(L0_wmod, dtype=np.float32))
    cst_f, cst_b = _make_consts()

    in_maps = []
    for r in range(n_cores):
        in_maps.append({
            "x": np.ascontiguousarray(x[r]),
            "wqkv": wqkv,
            "wout": wout,
            "wffn": wffn,
            "fbias": fbias,
            "c_row": np.ascontiguousarray(c_all[r:r + 1]),
            "wmod": wmod,
            "cst_f": cst_f,
            "cst_b": cst_b,
        })
    res = run_bass_kernel_spmd(nc, in_maps, core_ids=list(range(n_cores)))
    out = np.stack([np.asarray(res.results[r]["out"])
                    for r in range(n_cores)], axis=0)
    return out


# version 10 (final)
# speedup vs baseline: 1.1140x; 1.0151x over previous
import numpy as np
from contextlib import ExitStack

import concourse.bass as bass
import concourse.mybir as mybir
from concourse.bass_utils import run_bass_kernel_spmd

F32 = mybir.dt.float32
BF16 = mybir.dt.bfloat16
F8 = mybir.dt.float8e4
AT = mybir.AluOpType
ACT = mybir.ActivationFunctionType
AX = mybir.AxisListType

D = 1024
H = 16
NC = 8
TILE = 512
NSUB = 4
EPS = 1e-5
N_CORES = 8
INV_D = 1.0 / D


class Prog:
    def __init__(self, nc, es):
        self.sems = {}
        self.cnt = {}
        for name in ("tp", "vp", "sp", "gp"):
            self.sems[name] = es.enter_context(nc.semaphore(name))
            self.cnt[name] = 0

    def inc(self, name, instr):
        instr.then_inc(self.sems[name], 1)
        self.cnt[name] += 1
        return self.cnt[name]

    def wait(self, eng, name, n=None):
        eng.wait_ge(self.sems[name], self.cnt[name] if n is None else n)


class Dma:
    def __init__(self, nc, es, name):
        self.sem = es.enter_context(nc.semaphore(name))
        self.n = 0

    def inc(self, instr):
        instr.then_inc(self.sem, 16)
        self.n += 16
        return self.n

    def wait(self, eng, n=None):
        eng.wait_ge(self.sem, self.n if n is None else n)


def build_nc(n_tiles, n_cores=N_CORES, stage=99):
    S = n_tiles * TILE
    NST = S // 128
    nc = bass.Bass("TRN2", target_bir_lowering=False, debug=False,
                   detect_race_conditions=False)

    x_hbm = nc.dram_tensor("x", [S, D], F32, kind="ExternalInput").ap()
    wqkv_hbm = nc.dram_tensor("wqkv", [D, 3 * D], F32, kind="ExternalInput").ap()
    wout_hbm = nc.dram_tensor("wout", [D, D], F32, kind="ExternalInput").ap()
    wffn_hbm = nc.dram_tensor("wffn", [D, D], F32, kind="ExternalInput").ap()
    bias_hbm = nc.dram_tensor("fbias", [1, D], F32, kind="ExternalInput").ap()
    c_hbm = nc.dram_tensor("c_row", [1, D], F32, kind="ExternalInput").ap()
    wmod_hbm = nc.dram_tensor("wmod", [D, 6 * D], F32, kind="ExternalInput").ap()
    cstf_hbm = nc.dram_tensor("cst_f", [128, 128], F32, kind="ExternalInput").ap()
    cstb_hbm = nc.dram_tensor("cst_b", [772, 128], BF16, kind="ExternalInput").ap()
    out_hbm = nc.dram_tensor("out", [S, D], F32, kind="ExternalOutput").ap()
    dbg_hbm = nc.dram_tensor("dbg", [128, 8192], F32, kind="ExternalOutput").ap()
    cc_in = nc.dram_tensor("cc_in", [1, 64], F32)
    cc_out = nc.dram_tensor("cc_out", [1, 64], F32)

    es = ExitStack()
    sbt = lambda name, shp, dt: es.enter_context(nc.sbuf_tensor(name, shp, dt))

    x_sb = sbt("x_sb", [128, NST * D], F32)
    scr_bf = sbt("scr_bf", [128, 4096], BF16)
    qz_sb = sbt("qz_sb", [128, 2048], BF16)
    ca1_fm = sbt("ca1_fm", [128, NC], F32)
    sa_fm = sbt("sa_fm", [128, NC], F32)
    cb1_fm = sbt("cb1_fm", [128, NC], F32)
    sb_fm = sbt("sb_fm", [128, NC], F32)
    ga_b = sbt("ga_b", [128, D], BF16)
    gbrow = sbt("gbrow", [1, D], F32)
    biasrow = sbt("biasrow", [1, D], F32)
    biasq = sbt("biasq", [1, D], BF16)
    silu_f = sbt("silu_f", [128, NC], F32)
    silu_b = sbt("silu_b", [128, NC], BF16)
    silu_lo = sbt("silu_lo", [128, NC], BF16)
    ident = sbt("ident", [128, 128], F32)
    identb = sbt("identb", [128, 128], BF16)
    maskp = sbt("maskp", [128, 128], BF16)
    maskd = sbt("maskd", [128, 128], BF16)
    maskadd = sbt("maskadd", [128, 256], BF16)
    bdiag = sbt("bdiag", [128, 128], BF16)
    ones1 = sbt("ones1", [1, 128], BF16)
    negones1 = sbt("negones1", [1, 128], BF16)
    kmpair = sbt("kmpair", [16, 2], F32)
    km_use = sbt("km_use", [16, 1], F32)
    kmrow_bf = sbt("kmrow_bf", [1, 16], BF16)
    kmb16 = sbt("kmb16", [128, 16], F32)
    km128 = sbt("km128", [128, NC], BF16)
    brow4 = sbt("brow4", [1, 512], BF16)
    row512 = sbt("row512", [1, 512], F32)
    rinv = sbt("rinv", [128, 16], F32)
    rtmp = sbt("rtmp", [128, 16], F32)
    sxa4 = sbt("sxa4", [128, 4], F32)
    ssq4 = sbt("ssq4", [128, 4], F32)
    mean4 = sbt("mean4", [128, 4], F32)
    msq4 = sbt("msq4", [128, 4], F32)
    var4 = sbt("var4", [128, 4], F32)
    aa4 = sbt("aa4", [128, 4], F32)
    bb4 = sbt("bb4", [128, 4], F32)
    lntmp4 = sbt("lntmp4", [128, 4], F32)
    amax_run = sbt("amax_run", [128, 1], F32)
    amax16 = sbt("amax16", [128, n_tiles], F32)
    flv = sbt("flv", [128, 1], F32)
    fls = sbt("fls", [128, 1], F32)
    row128 = sbt("row128", [1, 128], F32)
    rowb = sbt("rowb", [1, 128], F32)
    sc1 = sbt("sc1", [1, 1], F32)
    sc2 = sbt("sc2", [1, 1], F32)
    sc3 = sbt("sc3", [1, 1], F32)
    amaxw_s = sbt("amaxw_s", [1, 1], F32)
    swh128 = sbt("swh128", [128, 1], F32)
    sxh128 = sbt("sxh128", [128, 1], F32)
    eps_c = sbt("eps_c", [128, 1], F32)
    tiny_c = sbt("tiny_c", [1, 1], F32)
    onescol = sbt("onescol", [128, 1], BF16)
    dbg_fill = sbt("dbg_fill", [128, 64], F32)
    ccin_s = sbt("ccin_s", [1, 64], F32)
    ccout_s = sbt("ccout_s", [1, 64], F32)

    PA = es.enter_context(nc.psum_tensor("PA", [128, 1024], F32))
    PB = es.enter_context(nc.psum_tensor("PB", [128, 1024], F32))
    PC = es.enter_context(nc.psum_tensor("PC", [128, 1024], F32))
    PD = es.enter_context(nc.psum_tensor("PD", [128, 1024], F32))

    pg = Prog(nc, es)
    d_x = Dma(nc, es, "d_x")
    d_wg = Dma(nc, es, "d_wg")
    d_ws = Dma(nc, es, "d_ws")
    d_out = Dma(nc, es, "d_out")
    d_cc = Dma(nc, es, "d_cc")
    cc_sem = es.enter_context(nc.semaphore("cc_sem"))

    blk = es.enter_context(nc.Block())

    def on(engine, fn):
        getattr(blk, engine)(fn)

    def r3(ap, b):
        return ap.rearrange("p (a b) -> p a b", b=b)

    open_cms = []

    def cm_enter(cm):
        open_cms.append(cm)
        return cm.__enter__()

    def cm_exit(cm):
        assert open_cms and open_cms[-1] is cm
        open_cms.pop()
        cm.__exit__(None, None, None)

    def finish_early():
        def f_fin(sy):
            pg.wait(sy, "vp")
            pg.wait(sy, "sp")
            pg.wait(sy, "tp")
            pg.wait(sy, "gp")
            d_x.wait(sy)
            d_ws.wait(sy)
            d_wg.wait(sy)
            d_cc.wait(sy)
            for t in range(n_tiles):
                d_out.inc(sy.dma_start(
                    out=out_hbm.rearrange("(s p) d -> p s d", p=128)
                    [:, t * NSUB:(t + 1) * NSUB, :],
                    in_=x_sb[:, t * NSUB * D:(t + 1) * NSUB * D]))
            pg.wait(sy, "gp", 1)
            import os as _os2
            dumped = (int(_os2.environ.get("KDSU", "-1")) >= 0)
            lo = 8160 if dumped else 0
            for c0 in range(lo, 8192, 64):
                w = min(64, 8192 - c0)
                d_out.inc(sy.dma_start(out=dbg_hbm[:, c0:c0 + w],
                                       in_=dbg_fill[:, 0:w]))
            d_out.wait(sy)
        on("sync", f_fin)

    def cut():
        finish_early()
        while open_cms:
            open_cms.pop().__exit__(None, None, None)
        es.close()
        return nc

    wqkv_cm = nc.sbuf_tensor("wqkv_sb", [128, NC * 3 * D], BF16)
    wqkv_sb = cm_enter(wqkv_cm)
    wout_cm = nc.sbuf_tensor("wout_sb", [128, NC * D], BF16)
    wout_sb = cm_enter(wout_cm)
    wmgA_cm = nc.sbuf_tensor("wmgA", [128, NC * 256], F32)
    wmgA = cm_enter(wmgA_cm)
    wmgB_cm = nc.sbuf_tensor("wmgB", [128, NC * 256], F32)
    wmgB = cm_enter(wmgB_cm)
    garow_cm = nc.sbuf_tensor("garow", [1, 2 * D], F32)
    garow = cm_enter(garow_cm)
    gclamp_cm = nc.sbuf_tensor("gclamp", [1, 2 * D], F32)
    gclamp = cm_enter(gclamp_cm)
    rowbuf_cm = nc.sbuf_tensor("rowbuf", [1, 256], F32)
    rowbuf = cm_enter(rowbuf_cm)
    crow_cm = nc.sbuf_tensor("crow", [1, D], F32)
    crow = cm_enter(crow_cm)

    def f_sy0(sy):
        d_ws.inc(sy.dma_start(out=ident[:], in_=cstf_hbm))
        d_ws.inc(sy.dma_start(out=identb[:], in_=cstb_hbm[0:128, :]))
        d_ws.inc(sy.dma_start(out=maskp[:], in_=cstb_hbm[128:256, :]))
        d_ws.inc(sy.dma_start(out=maskd[:], in_=cstb_hbm[256:384, :]))
        d_ws.inc(sy.dma_start(out=bdiag[:], in_=cstb_hbm[384:512, :]))
        d_ws.inc(sy.dma_start(out=ones1[:], in_=cstb_hbm[512:513, :]))
        d_ws.inc(sy.dma_start(out=negones1[:], in_=cstb_hbm[513:514, :]))
        d_ws.inc(sy.dma_start(out=maskadd[:, 0:128], in_=cstb_hbm[516:644, :]))
        d_ws.inc(sy.dma_start(out=maskadd[:, 128:256], in_=cstb_hbm[644:772, :]))
        d_ws.inc(sy.dma_start(out=onescol[:],
                              in_=cstb_hbm[512:513, :].rearrange("a (p b) -> p (a b)", p=128)))
        d_ws.inc(sy.dma_start(out=crow[:], in_=c_hbm))
        d_ws.inc(sy.dma_start(out=biasrow[:], in_=bias_hbm))
        for t in range(n_tiles):
            d_x.inc(sy.dma_start(
                out=x_sb[:, t * NSUB * D:(t + 1) * NSUB * D],
                in_=x_hbm.rearrange("(s p) d -> p s d", p=128)
                [:, t * NSUB:(t + 1) * NSUB, :]))
    on("sync", f_sy0)

    def f_g0(g):
        g.memset(kmpair[:], 0.0)
        g.memset(dbg_fill[:], 0.0)
        g.memset(ccin_s[:], 0.0)
        g.memset(eps_c[:], EPS)
        g.memset(flv[:], 0.0)
        g.memset(fls[:], 0.0)
        g.memset(qz_sb[:], 0.0)
        li = g.memset(tiny_c[:], 1e-30)
        pg.inc("gp", li)
        d_wg.inc(g.dma_start(
            out=r3(wqkv_sb[:], 3 * D),
            in_=wqkv_hbm.rearrange("(c p) f -> p c f", p=128)))
        d_wg.inc(g.dma_start(
            out=r3(wout_sb[:], D),
            in_=wout_hbm.rearrange("(c p) f -> p c f", p=128)))
    on("gpsimd", f_g0)

    def f_t0(t):
        pg.wait(t, "gp", 1)
        d_ws.wait(t)
        for c in range(NC):
            li = t.transpose(PD[:, c:c + 1], crow[0:1, c * 128:(c + 1) * 128],
                             ident[0:1, 0:1])
        pg.inc("tp", li)
    on("tensor", f_t0)

    def f_s0(s):
        pg.wait(s, "tp")
        s.activation(silu_f[:, 0:NC], PD[:, 0:NC], ACT.Silu)
        s.copy(fls[:], eps_c[:])
        li = s.copy(fls[:], eps_c[:])
        pg.inc("sp", li)
    on("scalar", f_s0)

    def f_v0(v):
        pg.wait(v, "sp")
        v.tensor_copy(silu_b[:], silu_f[:])
        v.tensor_copy(flv[:], eps_c[:])
        v.tensor_copy(flv[:], eps_c[:])
        v.tensor_tensor(silu_lo[:], silu_f[:], silu_b[:], AT.subtract)
        v.tensor_copy(flv[:], eps_c[:])
        li = v.tensor_copy(flv[:], eps_c[:])
        pg.inc("vp", li)
    on("vector", f_v0)

    pieces = ([(cc, False) for cc in range(0, 2048, 256)] +
              [(cc, False) for cc in range(3072, 5120, 256)] +
              [(cc, True) for cc in range(2048, 3072, 256)] +
              [(cc, True) for cc in range(5120, 6144, 256)])
    import os as _os
    KADIS = int(_os.environ.get("KADIS", "9"))
    _np_pieces = int(_os.environ.get("KPIECES", str(len(pieces))))
    pieces = pieces[:_np_pieces]
    buf_last_use = {0: 0, 1: 0}
    for pi, (col0, is32) in enumerate(pieces):
        bi = pi % 2
        buf = (wmgA, wmgB)[bi]
        slot = PB[0:1, bi * 512: bi * 512 + 256]

        def f_syp(sy, buf=buf, bi=bi, col0=col0):
            if buf_last_use[bi]:
                pg.wait(sy, "tp", buf_last_use[bi])
            d_ws.inc(sy.dma_start(
                out=r3(buf[:], 256),
                in_=wmod_hbm.rearrange("(c p) f -> p c f", p=128)[:, :, col0:col0 + 256]))
        on("sync", f_syp)

        def f_vc(v, buf=buf, is32=is32):
            d_ws.wait(v)
            pg.wait(v, "tp")
            li = v.tensor_copy(scr_bf[:, 0:2048], buf[:])
            if is32:
                li = v.tensor_tensor(scr_bf[:, 2048:4096], buf[:],
                                     scr_bf[:, 0:2048], AT.subtract)
            pg.inc("vp", li)
        on("vector", f_vc)

        def f_tm(t, slot=slot, is32=is32, pi=pi, bi=bi):
            pg.wait(t, "vp")
            if pi >= 2:
                pg.wait(t, "sp")
            chains = [(silu_b, 0)]
            if is32:
                chains = [(silu_b, 0), (silu_b, 2048), (silu_lo, 0)]
            n_mm = len(chains) * NC
            k = 0
            for lh, off in chains:
                for c in range(NC):
                    k += 1
                    li = t.matmul(slot, lh[:, c:c + 1],
                                  scr_bf[:, off + c * 256: off + (c + 1) * 256],
                                  start=(k == 1), stop=(k == n_mm))
            buf_last_use[bi] = pg.inc("tp", li)
        on("tensor", f_tm)

        if is32:
            ga_off = (col0 - 2048) if col0 < 3072 else (col0 - 5120 + 1024)

            def f_sev(s, slot=slot, ga_off=ga_off):
                pg.wait(s, "tp")
                li = s.copy(garow[0:1, ga_off:ga_off + 256], slot)
                pg.inc("sp", li)
            on("scalar", f_sev)
        else:
            vec = col0 // D
            d0 = col0 % D

            def f_sev(s, slot=slot):
                pg.wait(s, "tp")
                s.copy(rowbuf[:], slot)
                s.copy(fls[:], eps_c[:])
                li = s.copy(fls[:], eps_c[:])
                pg.inc("sp", li)
            on("scalar", f_sev)

            def f_tr(t):
                pg.wait(t, "sp")
                for i in range(2):
                    li = t.transpose(PD[:, 32 + i:33 + i],
                                     rowbuf[0:1, i * 128:(i + 1) * 128],
                                     ident[0:1, 0:1])
                pg.inc("tp", li)
            on("tensor", f_tr)

            dst, add1 = {0: (sa_fm, False), 1: (ca1_fm, True),
                         3: (sb_fm, False), 4: (cb1_fm, True)}[vec]

            def f_sev2(s, dst=dst, add1=add1, d0=d0):
                pg.wait(s, "tp")
                if add1:
                    li = s.add(dst[:, d0 // 128: d0 // 128 + 2], PD[:, 32:34], 1.0)
                else:
                    li = s.copy(dst[:, d0 // 128: d0 // 128 + 2], PD[:, 32:34])
                pg.inc("sp", li)
            on("scalar", f_sev2)

    if _np_pieces < 24 or int(_os.environ.get("KCUT2", "0")) == 1:
        return cut()

    def f_s1(s):
        pg.wait(s, "sp")
        s.copy(fls[:], eps_c[:])
        s.copy(fls[:], eps_c[:])
        li = s.activation(gclamp[:], garow[:], ACT.Abs)
        pg.inc("sp", li)
    on("scalar", f_s1)

    def f_v1(v):
        pg.wait(v, "sp")
        li = v.tensor_scalar_max(gclamp[:], gclamp[:], 0.25)
        pg.inc("vp", li)
    on("vector", f_v1)

    def f_s1b(s):
        pg.wait(s, "vp")
        li = s.activation(garow[:], garow[:], ACT.Sign)
        pg.inc("sp", li)
    on("scalar", f_s1b)

    def f_v1b(v):
        pg.wait(v, "sp")
        v.tensor_mul(garow[:], garow[:], gclamp[:])
        v.tensor_copy(gbrow[:], garow[0:1, D:2 * D])
        li = v.tensor_copy(scr_bf[0:1, 0:1024], garow[0:1, 0:D])
        pg.inc("vp", li)
    on("vector", f_v1b)

    def f_t1(t):
        pg.wait(t, "vp")
        t.matmul(PA[:, 0:512], ones1[:], scr_bf[0:1, 0:512], start=True, stop=True)
        li = t.matmul(PA[:, 512:1024], ones1[:], scr_bf[0:1, 512:1024],
                      start=True, stop=True)
        pg.inc("tp", li)
    on("tensor", f_t1)

    def f_s1c(s):
        pg.wait(s, "tp")
        li = s.copy(ga_b[:], PA[:, 0:1024])
        pg.inc("sp", li)
    on("scalar", f_s1c)

    if stage <= 2:
        return cut()

    cm_exit(crow_cm)
    cm_exit(rowbuf_cm)
    cm_exit(gclamp_cm)
    cm_exit(garow_cm)
    cm_exit(wmgB_cm)
    cm_exit(wmgA_cm)

    q_cm = nc.sbuf_tensor("q_sb", [128, NC * TILE], BF16)
    q_sb = cm_enter(q_cm)
    k_cm = nc.sbuf_tensor("k_sb", [128, NC * TILE], BF16)
    k_sb = cm_enter(k_cm)
    kp_cm = nc.sbuf_tensor("kp_sb", [128, NC * 128], BF16)
    kp_sb = cm_enter(kp_cm)
    v_cm = nc.sbuf_tensor("v_sb", [128, NSUB * H * 65], BF16)
    v_sb = cm_enter(v_cm)
    vp_cm = nc.sbuf_tensor("vp_sb", [128, H * 65], BF16)
    vp_sb = cm_enter(vp_cm)
    h_cm = nc.sbuf_tensor("h_sb", [128, NC * TILE], BF16)
    h_sb = cm_enter(h_cm)
    pt_cm = nc.sbuf_tensor("pt_sb", [128, H * 256], BF16)
    pt_sb = cm_enter(pt_cm)

    at_view = pt_sb[:, 0:H * 64]
    af_view = pt_sb[:, 2048:2048 + NC * 128]
    scr32 = scr_bf[:].bitcast(F32)
    tr32 = scr32[:, 1024:2048]
    qz = qz_sb[:]


    def f_vones(v):
        d_ws.wait(v)
        li = v.tensor_copy(
            r3(v_sb[:], 65)[:, :, 64:65],
            onescol[:].rearrange("p (a b) -> p a b", b=1)
            .broadcast_to([128, NSUB * H, 1]))
        pg.inc("vp", li)
    on("vector", f_vones)

    def xs_ln(ti, su):
        return x_sb[:, (ti * NSUB + su) * D:(ti * NSUB + su + 1) * D]

    for ti in range(n_tiles):
        sp0 = pg.cnt["sp"]
        vp0 = pg.cnt["vp"]

        def f_sln(s, vp0=vp0, sp0=sp0, ti=ti):
            d_x.wait(s, 16 * (ti + 1))
            pg.wait(s, "tp")
            pg.wait(s, "vp")
            for su in range(NSUB):
                xs = x_sb[:, (ti * NSUB + su) * D:(ti * NSUB + su + 1) * D]
                li = s.activation(scr_bf[:, su * D:(su + 1) * D], xs, ACT.Square,
                                  accum_out=ssq4[:, su:su + 1])
            pg.inc("sp", li)
            pg.wait(s, "vp", vp0 + 1)
            for su in range(NSUB):
                s.activation(lntmp4[:, su:su + 1], var4[:, su:su + 1],
                             ACT.Ln, bias=eps_c[:])
            for su in range(NSUB):
                li = s.activation(aa4[:, su:su + 1], lntmp4[:, su:su + 1],
                                  ACT.Exp, scale=-0.5)
            pg.inc("sp", li)
        on("scalar", f_sln)

        def f_vln(v, ti=ti, sp0=sp0):
            d_x.wait(v, 16 * (ti + 1))
            for su in range(NSUB):
                xs = x_sb[:, (ti * NSUB + su) * D:(ti * NSUB + su + 1) * D]
                v.reduce_sum(sxa4[:, su:su + 1], xs, axis=AX.X)
            for su in range(NSUB):
                v.tensor_scalar_mul(mean4[:, su:su + 1], sxa4[:, su:su + 1], INV_D)
            for su in range(NSUB):
                v.tensor_tensor(msq4[:, su:su + 1], mean4[:, su:su + 1],
                                mean4[:, su:su + 1], AT.mult)
            pg.wait(v, "sp", sp0 + 1)
            for su in range(NSUB):
                li = v.tensor_scalar(var4[:, su:su + 1], ssq4[:, su:su + 1],
                                     INV_D, msq4[:, su:su + 1],
                                     AT.mult, AT.subtract)
            pg.inc("vp", li)
            pg.wait(v, "sp", sp0 + 2)
            for su in range(NSUB):
                v.tensor_scalar(bb4[:, su:su + 1], mean4[:, su:su + 1],
                                aa4[:, su:su + 1], -1.0, AT.mult, AT.mult)
            for su in range(NSUB):
                li = v.tensor_scalar(scr_bf[:, su * D:(su + 1) * D], xs_ln(ti, su),
                                     aa4[:, su:su + 1], bb4[:, su:su + 1],
                                     AT.mult, AT.add)
            pg.inc("vp", li)
        on("vector", f_vln)

        for rnd in range(4):
            reg = (PA, PB)[rnd % 2]

            def f_tt(t, rnd=rnd, reg=reg):
                pg.wait(t, "vp")
                pg.wait(t, "sp")
                regb = reg[:].bitcast(BF16)
                for cl, c in enumerate((2 * rnd, 2 * rnd + 1)):
                    for su in range(NSUB):
                        li = t.transpose(
                            regb[:, cl * 512 + su * 128: cl * 512 + (su + 1) * 128],
                            scr_bf[:, su * D + c * 128: su * D + (c + 1) * 128],
                            identb[:])
                pg.inc("tp", li)
            on("tensor", f_tt)

            def f_st(s, rnd=rnd, reg=reg):
                pg.wait(s, "tp")
                regb = reg[:].bitcast(BF16)
                for cl, c in enumerate((2 * rnd, 2 * rnd + 1)):
                    li = s.activation(h_sb[:, c * 512:(c + 1) * 512],
                                      regb[:, cl * 512:(cl + 1) * 512], ACT.Identity,
                                      bias=sa_fm[:, c:c + 1], scale=ca1_fm[:, c:c + 1])
                pg.inc("sp", li)
            on("scalar", f_st)

        for part, f0, dst, scale in (("q", 0, q_sb, 0.125), ("k", D, k_sb, None)):
            for fc in range(NC):
                half = (PA, PB)[fc % 2]

                def f_tqk(t, fc=fc, f0=f0, half=half, part=part, ti=ti):
                    if part == "q" and fc == 0:
                        pg.wait(t, "sp")
                        if ti == 0:
                            d_wg.wait(t)
                    pg.wait(t, "vp")
                    for dcc in range(NC):
                        li = t.matmul(half[:, 0:512],
                                      wqkv_sb[:, dcc * 3 * D + f0 + fc * 128:
                                              dcc * 3 * D + f0 + (fc + 1) * 128],
                                      h_sb[:, dcc * 512:(dcc + 1) * 512],
                                      start=(dcc == 0), stop=(dcc == NC - 1))
                    pg.inc("tp", li)
                on("tensor", f_tqk)

                def f_vqk(v, fc=fc, dst=dst, half=half, scale=scale):
                    pg.wait(v, "tp")
                    if scale is not None:
                        li = v.tensor_scalar(dst[:, fc * 512:(fc + 1) * 512],
                                             half[:, 0:512], scale, None, AT.mult)
                    else:
                        li = v.tensor_copy(dst[:, fc * 512:(fc + 1) * 512],
                                           half[:, 0:512])
                    pg.inc("vp", li)
                on("vector", f_vqk)

        for su in range(NSUB):
            half = (PA, PB)[su % 2]

            def f_tv(t, su=su, half=half):
                pg.wait(t, "vp")
                for dcc in range(NC):
                    lh = h_sb[:, dcc * 512 + su * 128: dcc * 512 + (su + 1) * 128]
                    t.matmul(half[:, 0:512], lh,
                             wqkv_sb[:, dcc * 3 * D + 2 * D: dcc * 3 * D + 2 * D + 512],
                             start=(dcc == 0), stop=(dcc == NC - 1))
                    li = t.matmul(half[:, 512:1024], lh,
                                  wqkv_sb[:, dcc * 3 * D + 2 * D + 512: dcc * 3 * D + 3 * D],
                                  start=(dcc == 0), stop=(dcc == NC - 1))
                pg.inc("tp", li)
            on("tensor", f_tv)

            def f_vv(v, su=su, half=half):
                pg.wait(v, "tp")
                li = v.tensor_copy(
                    r3(v_sb[:, su * H * 65:(su + 1) * H * 65], 65)[:, :, 0:64],
                    r3(half[:, 0:1024], 64))
                pg.inc("vp", li)
            on("vector", f_vv)

        def f_sk(s):
            pg.wait(s, "vp")
            s.activation(scr_bf[:, 0:2048], k_sb[:, 0:2048], ACT.Square)
            li = s.activation(scr_bf[:, 2048:4096], k_sb[:, 2048:4096], ACT.Square)
            pg.inc("sp", li)
        on("scalar", f_sk)

        def f_tk(t):
            pg.wait(t, "sp")
            for c in range(NC):
                li = t.matmul(PD[0:16, 0:512], bdiag[:, c * 16:(c + 1) * 16],
                              scr_bf[:, c * 512:(c + 1) * 512],
                              start=(c == 0), stop=(c == NC - 1))
            pg.inc("tp", li)
        on("tensor", f_tk)

        def f_vk(v, ti=ti):
            pg.wait(v, "tp")
            if ti == 0:
                pg.wait(v, "gp", 1)
            v.reduce_max(kmpair[:, 1:2], PD[0:16, 0:512], axis=AX.X)
            v.tensor_copy(flv[:], eps_c[:])
            v.tensor_copy(flv[:], eps_c[:])
            v.reduce_max(km_use[:], kmpair[:, 0:2], axis=AX.X)
            v.tensor_copy(kmpair[:, 0:1], kmpair[:, 1:2])
            li = v.tensor_copy(flv[:], eps_c[:])
            pg.inc("vp", li)
        on("vector", f_vk)

        def f_tk2(t):
            pg.wait(t, "vp")
            li = t.transpose(PD[0:1, 512:528], km_use[:], ident[0:16, 0:16])
            pg.inc("tp", li)
        on("tensor", f_tk2)

        def f_sk2(s):
            pg.wait(s, "tp")
            s.activation(kmrow_bf[:], PD[0:1, 512:528], ACT.Identity)
            s.copy(fls[:], eps_c[:])
            li = s.copy(fls[:], eps_c[:])
            pg.inc("sp", li)
        on("scalar", f_sk2)

        def f_tk4(t):
            pg.wait(t, "sp")
            li = t.matmul(PD[:, 528:544], ones1[:], kmrow_bf[:],
                          start=True, stop=True)
            pg.inc("tp", li)
        on("tensor", f_tk4)

        def f_vk3(v):
            pg.wait(v, "tp")
            v.tensor_copy(kmb16[:], PD[:, 528:544])
            v.tensor_copy(flv[:], eps_c[:])
            v.tensor_copy(km128[0:64, :], r3(kmb16[0:64, :], 2)[:, :, 0:1])
            v.tensor_copy(km128[64:128, :], r3(kmb16[64:128, :], 2)[:, :, 1:2])
            v.tensor_copy(flv[:], eps_c[:])
            li = v.tensor_copy(flv[:], eps_c[:])
            pg.inc("vp", li)
        on("vector", f_vk3)

        def f_sq4(s):
            pg.wait(s, "tp")
            li = s.activation(scr_bf[:, 0:4096], q_sb[:, 0:4096], ACT.Square)
            pg.inc("sp", li)
        on("scalar", f_sq4)

        def f_tb4(t):
            pg.wait(t, "vp")
            pg.wait(t, "sp")
            for su in range(NSUB):
                for c in range(NC):
                    li = t.matmul(PD[0:1, 512 + su * 128: 640 + su * 128],
                                  km128[:, c:c + 1],
                                  scr_bf[:, c * 512 + su * 128:
                                         c * 512 + (su + 1) * 128],
                                  start=(c == 0), stop=(c == NC - 1))
            pg.inc("tp", li)
        on("tensor", f_tb4)

        def f_sb24(s):
            pg.wait(s, "tp")
            s.activation(row512[:], PD[0:1, 512:1024], ACT.Ln, bias=tiny_c[:])
            s.copy(fls[:], eps_c[:])
            s.copy(fls[:], eps_c[:])
            s.activation(brow4[:], row512[:], ACT.Exp, scale=0.5)
            s.copy(fls[:], eps_c[:])
            li = s.copy(fls[:], eps_c[:])
            pg.inc("sp", li)
        on("scalar", f_sb24)

        def mk_vqz(su, ti=ti):
            def f_vqz(v, su=su, ti=ti):
                pg.wait(v, "tp")
                if ti == 0 and su == 0:
                    pg.wait(v, "gp", 1)
                qz4 = qz.rearrange("p (a b c) -> p a b c", b=2, c=128)
                v.tensor_copy(qz4[0:64, :, 0, :],
                              r3(q_sb[0:64, :], 512)[:, :, su * 128:(su + 1) * 128])
                li = v.tensor_copy(qz4[64:128, :, 1, :],
                                   r3(q_sb[64:128, :], 512)[:, :, su * 128:(su + 1) * 128])
                pg.inc("vp", li)
            on("vector", f_vqz)
            return pg.cnt["vp"]

        vq_vp = {0: mk_vqz(0)}
        for su in range(NSUB):
            if KADIS < 1:
                break
            first_global = (ti == 0 and su == 0)

            if KADIS < 2:
                continue
            for g4 in range(4):
                reg = (PB, PC)[g4 % 2]

                def f_ts(t, g4=g4, reg=reg, su=su, first_global=first_global,
                         vq=None):
                    pg.wait(t, "sp")
                    pg.wait(t, "vp", vq_vp[su])
                    bspan = brow4[0:1, su * 128:(su + 1) * 128] \
                        .rearrange("p (a b c) -> p a b c", b=1, c=128) \
                        .broadcast_to([1, 2, 2, 128])
                    qz4 = qz.rearrange("p (a b c) -> p a b c", b=2, c=128)
                    madd = maskadd[:].rearrange("p (a b t) -> p a b t", a=1, b=2) \
                        .broadcast_to([128, 2, 2, 128])
                    for bank in range(2):
                        out4 = r3(reg[:, bank * 512:(bank + 1) * 512], 128) \
                            .rearrange("p (a b) c -> p a b c", b=2)
                        t.matmul(out4, identb[:], madd, start=True, stop=False)
                        t.matmul(out4, negones1[:], bspan, start=False, stop=False)
                        for hh in range(2):
                            h = g4 * 4 + bank * 2 + hh
                            par = h % 2
                            pr = h // 2
                            qrhs = qz4[:, pr, par, :]
                            col0 = bank * 512 + hh * 256
                            if not first_global:
                                if su == 0:
                                    klh = kp_sb[:, pr * 128:(pr + 1) * 128]
                                else:
                                    klh = k_sb[:, pr * 512 + (su - 1) * 128:
                                               pr * 512 + su * 128]
                                t.matmul(reg[:, col0:col0 + 128], klh, qrhs,
                                         start=False, stop=False)
                            li = t.matmul(reg[:, col0 + 128:col0 + 256],
                                          k_sb[:, pr * 512 + su * 128:
                                               pr * 512 + (su + 1) * 128],
                                          qrhs, start=False, stop=(hh == 1))
                    pg.inc("tp", li)
                on("tensor", f_ts)

                def f_se(s, g4=g4, reg=reg):
                    pg.wait(s, "tp")
                    li = s.activation(pt_sb[:, g4 * 1024:(g4 + 1) * 1024],
                                      reg[:], ACT.Exp)
                    pg.inc("sp", li)
                on("scalar", f_se)

            if KADIS < 3:
                continue

            def f_tpv(t, su=su, first_global=first_global):
                pg.wait(t, "sp")
                pg.wait(t, "vp")
                for rr in range(2):
                    reg = (PD, PC)[rr]
                    for hh in range(8):
                        h = rr * 8 + hh
                        outp = reg[:, hh * 128: hh * 128 + 65]
                        started = False
                        if not first_global:
                            if su == 0:
                                vrhs = vp_sb[:, h * 65:(h + 1) * 65]
                            else:
                                vrhs = v_sb[:, ((su - 1) * H + h) * 65:
                                            ((su - 1) * H + h + 1) * 65]
                            t.matmul(outp, pt_sb[:, h * 256: h * 256 + 128], vrhs,
                                     start=True, stop=False)
                            started = True
                        li = t.matmul(outp, pt_sb[:, h * 256 + 128: h * 256 + 256],
                                      v_sb[:, (su * H + h) * 65:(su * H + h + 1) * 65],
                                      start=not started, stop=True)
                pg.inc("tp", li)
            on("tensor", f_tpv)

            def f_vr(v):
                pg.wait(v, "tp")
                pd3 = r3(PD[:], 128)
                pc3 = r3(PC[:], 128)
                v.tensor_scalar_add(rtmp[:, 0:8], pd3[:, 0:8, 64:65], 1e-30)
                v.tensor_scalar_add(rtmp[:, 8:16], pc3[:, 0:8, 64:65], 1e-30)
                v.reciprocal(rinv[:, 0:8], rtmp[:, 0:8])
                v.reciprocal(rinv[:, 8:16], rtmp[:, 8:16])
                for rr, p3 in ((0, pd3), (1, pc3)):
                    li = v.tensor_tensor(
                        r3(at_view[:, rr * 512:(rr + 1) * 512], 64),
                        p3[:, 0:8, 0:64],
                        rinv[:, rr * 8:(rr + 1) * 8]
                        .rearrange("p (a b) -> p a b", b=1).broadcast_to([128, 8, 64]),
                        AT.mult)
                pg.inc("vp", li)
            on("vector", f_vr)

            if KADIS < 4:
                continue

            def f_ta(t):
                pg.wait(t, "vp")
                pbb = PB[:].bitcast(BF16)
                for c in range(NC):
                    li = t.transpose(pbb[:, c * 128:(c + 1) * 128],
                                     at_view[:, c * 128:(c + 1) * 128], identb[:])
                pg.inc("tp", li)
            on("tensor", f_ta)

            def f_va(v):
                pg.wait(v, "tp")
                li = v.tensor_copy(af_view[:], PB[:].bitcast(BF16)[:, 0:1024])
                pg.inc("vp", li)
            on("vector", f_va)

            if su + 1 < NSUB:
                vq_vp[su + 1] = mk_vqz(su + 1)

            KDSU = int(_os.environ.get("KDSU", "-1"))
            if ti == 0 and su == KDSU:
                def f_dbg(g, su=su):
                    pg.wait(g, "vp")
                    pg.wait(g, "sp")
                    d_cc.inc(g.dma_start(out=dbg_hbm[:, 0:4096], in_=pt_sb[:]))
                    d_cc.inc(g.dma_start(out=dbg_hbm[:, 4096:4112], in_=rinv[:]))
                    d_cc.inc(g.dma_start(out=dbg_hbm[0:1, 4240:4368],
                                          in_=brow4[0:1, su * 128:(su + 1) * 128]))
                    d_cc.inc(g.dma_start(out=dbg_hbm[:, 4368:4376], in_=km128[:]))
                    d_cc.inc(g.dma_start(out=dbg_hbm[:, 4376:4392], in_=kmb16[:]))
                    d_cc.inc(g.dma_start(out=dbg_hbm[0:16, 4392:4393], in_=km_use[:], allow_slow_non_contiguous=True))
                    d_cc.inc(g.dma_start(out=dbg_hbm[0:16, 4393:4395], in_=kmpair[:], allow_slow_non_contiguous=True))
                    d_cc.inc(g.dma_start(out=dbg_hbm[:, 4416:5440],
                                          in_=scr_bf[:, 0:1024]))
                    d_cc.inc(g.dma_start(out=dbg_hbm[:, 5440:6464],
                                          in_=q_sb[:, 0:1024]))
                    d_cc.inc(g.dma_start(out=dbg_hbm[:, 6464:7488],
                                          in_=k_sb[:, 0:1024]))
                    d_cc.inc(g.dma_start(out=dbg_hbm[0:1, 7488:7489], in_=tiny_c[:]))
                    d_cc.inc(g.dma_start(out=dbg_hbm[:, 7489:7497],
                                          in_=silu_f[:, 0:8]))
                    d_cc.inc(g.dma_start(out=dbg_hbm[:, 7497:7505],
                                          in_=ca1_fm[:]))
                    d_cc.inc(g.dma_start(out=dbg_hbm[:, 7505:7513],
                                          in_=km128[:].bitcast(BF16)[:, 0:8]))
                    d_cc.inc(g.dma_start(out=dbg_hbm[:, 7520:8032],
                                          in_=h_sb[:, 0:512]))
                    d_cc.inc(g.dma_start(out=dbg_hbm[:, 8032:8096],
                                          in_=x_sb[:, 0:64]))
                    d_cc.inc(g.dma_start(out=dbg_hbm[:, 8096:8160],
                                          in_=wqkv_sb[:, 0:64]))
                on("gpsimd", f_dbg)

            if KADIS < 5:
                continue

            def f_tp(t):
                pg.wait(t, "vp")
                for dcc in range(NC):
                    lh = af_view[:, dcc * 128:(dcc + 1) * 128]
                    t.matmul(PA[:, 0:512], lh, wout_sb[:, dcc * D: dcc * D + 512],
                             start=(dcc == 0), stop=(dcc == NC - 1))
                    li = t.matmul(PA[:, 512:1024], lh,
                                  wout_sb[:, dcc * D + 512:(dcc + 1) * D],
                                  start=(dcc == 0), stop=(dcc == NC - 1))
                pg.inc("tp", li)
            on("tensor", f_tp)

            if KADIS < 6:
                continue

            def f_vres(v, ti=ti, su=su):
                pg.wait(v, "tp")
                xs = x_sb[:, (ti * NSUB + su) * D:(ti * NSUB + su + 1) * D]
                v.tensor_tensor(tr32, PA[:, 0:1024], ga_b[:], AT.mult)
                li = v.tensor_add(xs, xs, tr32)
                pg.inc("vp", li)
            on("vector", f_vres)

            if int(_os.environ.get("KONESU", "0")) and ti == 0 and su == 0:
                return cut()

        def f_vring(v):
            v.tensor_copy(r3(kp_sb[:], 128), r3(k_sb[:], 512)[:, :, 384:512])
            li = v.tensor_copy(vp_sb[:], v_sb[:, 3 * H * 65:4 * H * 65])
            pg.inc("vp", li)
        on("vector", f_vring)

    if stage <= 4:
        return cut()

    cm_exit(pt_cm)
    cm_exit(h_cm)
    cm_exit(vp_cm)
    cm_exit(v_cm)
    cm_exit(kp_cm)
    cm_exit(k_cm)
    cm_exit(q_cm)
    cm_exit(wout_cm)
    cm_exit(wqkv_cm)
    h2_cm = nc.sbuf_tensor("h2_sb", [128, NC * S], BF16)
    h2_sb = cm_enter(h2_cm)

    a2_tp = pg.cnt["tp"]
    a2_vp = pg.cnt["vp"]
    a2_sp = pg.cnt["sp"]

    for ti in range(n_tiles):
        sp0 = pg.cnt["sp"]
        vp0 = pg.cnt["vp"]

        def f_sln2(s, vp0=vp0, sp0=sp0, ti=ti):
            pg.wait(s, "tp")
            if ti == 0:
                pg.wait(s, "vp")
            for su in range(NSUB):
                xs = x_sb[:, (ti * NSUB + su) * D:(ti * NSUB + su + 1) * D]
                li = s.activation(scr_bf[:, su * D:(su + 1) * D], xs, ACT.Square,
                                  accum_out=ssq4[:, su:su + 1])
            pg.inc("sp", li)
            pg.wait(s, "vp", vp0 + 1)
            for su in range(NSUB):
                s.activation(lntmp4[:, su:su + 1], var4[:, su:su + 1],
                             ACT.Ln, bias=eps_c[:])
            for su in range(NSUB):
                li = s.activation(aa4[:, su:su + 1], lntmp4[:, su:su + 1],
                                  ACT.Exp, scale=-0.5)
            pg.inc("sp", li)
        on("scalar", f_sln2)

        def f_vln2(v, ti=ti, sp0=sp0):
            for su in range(NSUB):
                xs = x_sb[:, (ti * NSUB + su) * D:(ti * NSUB + su + 1) * D]
                v.reduce_sum(sxa4[:, su:su + 1], xs, axis=AX.X)
            for su in range(NSUB):
                v.tensor_scalar_mul(mean4[:, su:su + 1], sxa4[:, su:su + 1], INV_D)
            for su in range(NSUB):
                v.tensor_tensor(msq4[:, su:su + 1], mean4[:, su:su + 1],
                                mean4[:, su:su + 1], AT.mult)
            pg.wait(v, "sp", sp0 + 1)
            for su in range(NSUB):
                li = v.tensor_scalar(var4[:, su:su + 1], ssq4[:, su:su + 1],
                                     INV_D, msq4[:, su:su + 1],
                                     AT.mult, AT.subtract)
            pg.inc("vp", li)
            pg.wait(v, "sp", sp0 + 2)
            for su in range(NSUB):
                v.tensor_scalar(bb4[:, su:su + 1], mean4[:, su:su + 1],
                                aa4[:, su:su + 1], -1.0, AT.mult, AT.mult)
            for su in range(NSUB):
                li = v.tensor_scalar(scr_bf[:, su * D:(su + 1) * D], xs_ln(ti, su),
                                     aa4[:, su:su + 1], bb4[:, su:su + 1],
                                     AT.mult, AT.add)
            pg.inc("vp", li)
        on("vector", f_vln2)

        for rnd in range(4):
            reg = (PA, PB)[rnd % 2]

            def f_tt2(t, rnd=rnd, reg=reg):
                pg.wait(t, "vp")
                pg.wait(t, "sp")
                regb = reg[:].bitcast(BF16)
                for cl, c in enumerate((2 * rnd, 2 * rnd + 1)):
                    for su in range(NSUB):
                        li = t.transpose(
                            regb[:, cl * 512 + su * 128: cl * 512 + (su + 1) * 128],
                            scr_bf[:, su * D + c * 128: su * D + (c + 1) * 128],
                            identb[:])
                pg.inc("tp", li)
            on("tensor", f_tt2)

            def f_st2(s, rnd=rnd, reg=reg, ti=ti):
                pg.wait(s, "tp")
                regb = reg[:].bitcast(BF16)
                for cl, c in enumerate((2 * rnd, 2 * rnd + 1)):
                    li = s.activation(h2_sb[:, c * S + ti * 512: c * S + (ti + 1) * 512],
                                      regb[:, cl * 512:(cl + 1) * 512], ACT.Identity,
                                      bias=sb_fm[:, c:c + 1], scale=cb1_fm[:, c:c + 1])
                pg.inc("sp", li)
            on("scalar", f_st2)

        def f_vam(v, ti=ti):
            pg.wait(v, "sp")
            li = v.reduce_max(amax16[:, ti:ti + 1],
                              h2_sb[:].rearrange("p (c t) -> p c t", t=S)
                              [:, :, ti * 512:(ti + 1) * 512],
                              axis=AX.XY, apply_absolute_value=True)
            pg.inc("vp", li)
        on("vector", f_vam)

    wq8_cm = nc.sbuf_tensor("wq8_sb", [128, NC * D], F8)
    wq8_sb = cm_enter(wq8_cm)
    wst_cm = nc.sbuf_tensor("wst", [128, NC * D], F32)
    wst = cm_enter(wst_cm)

    def f_sy2(sy):
        pg.wait(sy, "tp", a2_tp)
        pg.wait(sy, "vp", a2_vp)
        pg.wait(sy, "sp", a2_sp)
        d_ws.inc(sy.dma_start(
            out=r3(wst[:], D),
            in_=wffn_hbm.rearrange("(c p) f -> p c f", p=128)))
    on("sync", f_sy2)

    def f_v2(v):
        d_ws.wait(v)
        v.reduce_max(kmb16[:, 0:1], r3(wst[:], D), axis=AX.XY,
                     apply_absolute_value=True)
        v.tensor_copy(flv[:], eps_c[:])
        li = v.tensor_copy(flv[:], eps_c[:])
        pg.inc("vp", li)
    on("vector", f_v2)

    def f_t2(t):
        pg.wait(t, "vp")
        li = t.transpose(PD[0:1, 48:176], kmb16[:, 0:1], ident[:])
        pg.inc("tp", li)
    on("tensor", f_t2)

    def f_s2(s):
        pg.wait(s, "tp")
        s.copy(row128[:], PD[0:1, 48:176])
        s.copy(fls[:], eps_c[:])
        li = s.copy(fls[:], eps_c[:])
        pg.inc("sp", li)
    on("scalar", f_s2)

    def f_v2b(v):
        pg.wait(v, "sp")
        v.reduce_max(amaxw_s[:], row128[:], axis=AX.X)
        v.tensor_copy(flv[:], eps_c[:])
        v.tensor_copy(flv[:], eps_c[:])
        v.tensor_scalar_max(amaxw_s[:], amaxw_s[:], 1e-12)
        v.tensor_copy(flv[:], eps_c[:])
        v.tensor_copy(flv[:], eps_c[:])
        v.reciprocal(sc1[:], amaxw_s[:])
        v.tensor_copy(flv[:], eps_c[:])
        v.tensor_copy(flv[:], eps_c[:])
        v.tensor_scalar_mul(sc1[:], sc1[:], 224.0)
        v.tensor_copy(flv[:], eps_c[:])
        v.tensor_copy(flv[:], eps_c[:])
        v.tensor_copy(rowb[:], sc1[0:1, 0:1].rearrange("p (a b) -> p a b", b=1)
                      .broadcast_to([1, 128, 1]))
        v.tensor_copy(flv[:], eps_c[:])
        li = v.tensor_copy(flv[:], eps_c[:])
        pg.inc("vp", li)
    on("vector", f_v2b)

    def f_t2b(t):
        pg.wait(t, "vp")
        li = t.transpose(PD[:, 176:177], rowb[0:1, :], ident[0:1, 0:1])
        pg.inc("tp", li)
    on("tensor", f_t2b)

    def f_s2b(s):
        pg.wait(s, "tp")
        s.copy(swh128[:], PD[:, 176:177])
        s.copy(fls[:], eps_c[:])
        li = s.copy(fls[:], eps_c[:])
        pg.inc("sp", li)
    on("scalar", f_s2b)

    for dc in range(NC):
        def f_tq(t, dc=dc):
            pg.wait(t, "sp")
            pg.wait(t, "vp")
            for oc in range(NC):
                li = t.transpose(PA[:, oc * 128:(oc + 1) * 128],
                                 wst[:, oc * D + dc * 128: oc * D + (dc + 1) * 128],
                                 ident[:])
            pg.inc("tp", li)
        on("tensor", f_tq)

        def f_vq(v, dc=dc):
            pg.wait(v, "tp")
            li = v.tensor_scalar(wq8_sb[:, dc * D:(dc + 1) * D], PA[:, 0:1024],
                                 swh128[:], None, AT.mult)
            pg.inc("vp", li)
        on("vector", f_vq)

    cm_exit(wst_cm)

    if stage <= 3:
        return cut()


    def f_vamf(v):
        v.tensor_copy(flv[:], eps_c[:])
        v.tensor_copy(flv[:], eps_c[:])
        v.reduce_max(amax_run[:], amax16[:], axis=AX.X)
        v.tensor_copy(flv[:], eps_c[:])
        li = v.tensor_copy(flv[:], eps_c[:])
        pg.inc("vp", li)
    on("vector", f_vamf)

    def f_tam(t):
        pg.wait(t, "vp")
        li = t.transpose(PD[0:1, 768:896], amax_run[:], ident[:])
        pg.inc("tp", li)
    on("tensor", f_tam)

    def f_sam(s):
        pg.wait(s, "tp")
        s.copy(row128[:], PD[0:1, 768:896])
        s.copy(fls[:], eps_c[:])
        li = s.copy(fls[:], eps_c[:])
        pg.inc("sp", li)
    on("scalar", f_sam)

    def f_vam2(v):
        pg.wait(v, "sp")
        v.reduce_max(sc1[:], row128[:], axis=AX.X)
        v.tensor_copy(flv[:], eps_c[:])
        v.tensor_copy(flv[:], eps_c[:])
        v.tensor_scalar_max(sc1[:], sc1[:], 1e-12)
        v.tensor_copy(flv[:], eps_c[:])
        v.tensor_copy(flv[:], eps_c[:])
        v.tensor_copy(ccin_s[0:1, 0:1], sc1[:])
        v.tensor_copy(flv[:], eps_c[:])
        li = v.tensor_copy(flv[:], eps_c[:])
        pg.inc("vp", li)
    on("vector", f_vam2)

    def f_sycc(sy):
        pg.wait(sy, "vp")
        d_cc.inc(sy.dma_start(out=cc_in.ap(), in_=ccin_s[:]))
    on("sync", f_sycc)

    def f_gcc(g):
        d_cc.wait(g)
        g.collective_compute(
            "AllReduce", AT.max, replica_groups=[list(range(n_cores))],
            ins=[cc_in.ap().opt()], outs=[cc_out.ap().opt()]).then_inc(cc_sem, 1)
    on("gpsimd", f_gcc)

    def f_sycc2(sy):
        sy.wait_ge(cc_sem, 1)
        d_cc.inc(sy.dma_start(out=ccout_s[:], in_=cc_out.ap()))
    on("sync", f_sycc2)

    if stage <= 5:
        return cut()

    g1_cm = nc.sbuf_tensor("g1_b", [128, D], BF16)
    g1_b = cm_enter(g1_cm)
    xq_cm = nc.sbuf_tensor("xq_sb", [128, NC * TILE], F8)
    xq_sb = cm_enter(xq_cm)
    gbc_cm = nc.sbuf_tensor("gbc_row", [1, D], BF16)
    gbc_row = cm_enter(gbc_cm)

    def f_vg(v):
        d_cc.wait(v)
        v.reciprocal(sc2[:], ccout_s[0:1, 0:1])
        v.tensor_copy(flv[:], eps_c[:])
        v.tensor_copy(flv[:], eps_c[:])
        v.tensor_scalar_mul(sc2[:], sc2[:], 224.0)
        v.tensor_copy(flv[:], eps_c[:])
        v.tensor_copy(flv[:], eps_c[:])
        v.tensor_copy(rowb[:], sc2[0:1, 0:1].rearrange("p (a b) -> p a b", b=1)
                      .broadcast_to([1, 128, 1]))
        v.tensor_mul(sc2[:], ccout_s[0:1, 0:1], amaxw_s[:])
        v.tensor_copy(flv[:], eps_c[:])
        v.tensor_copy(flv[:], eps_c[:])
        v.tensor_scalar_mul(sc2[:], sc2[:], 1.0 / 50176.0)
        v.tensor_copy(flv[:], eps_c[:])
        v.tensor_copy(flv[:], eps_c[:])
        v.reciprocal(sc3[:], sc2[:])
        v.tensor_copy(flv[:], eps_c[:])
        v.tensor_copy(flv[:], eps_c[:])
        v.tensor_scalar(biasq[:], biasrow[:], sc3[:], None, AT.mult)
        li = v.tensor_scalar(gbc_row[:], gbrow[:], sc2[:], None, AT.mult)
        pg.inc("vp", li)
    on("vector", f_vg)

    def f_tg(t):
        pg.wait(t, "vp")
        t.transpose(PD[:, 896:897], rowb[0:1, :], ident[0:1, 0:1])
        t.matmul(PA[:, 0:512], ones1[:], gbc_row[0:1, 0:512], start=True, stop=True)
        li = t.matmul(PA[:, 512:1024], ones1[:], gbc_row[0:1, 512:1024],
                      start=True, stop=True)
        pg.inc("tp", li)
    on("tensor", f_tg)

    def f_sg(s):
        pg.wait(s, "tp")
        s.copy(sxh128[:], PD[:, 896:897])
        li = s.copy(g1_b[:], PA[:, 0:1024])
        pg.inc("sp", li)
    on("scalar", f_sg)

    for ti in range(n_tiles):
        def f_vxq(v, ti=ti):
            pg.wait(v, "vp")
            pg.wait(v, "sp")
            li = v.tensor_scalar(
                r3(xq_sb[:], 512),
                h2_sb[:].rearrange("p (c t) -> p c t", t=S)
                [:, :, ti * 512:(ti + 1) * 512],
                sxh128[:], None, AT.mult)
            pg.inc("vp", li)
        on("vector", f_vxq)

        vpB = pg.cnt["vp"]
        for su in range(NSUB):
            regf = (PA, PB)[su % 2]

            def f_tf(t, su=su, regf=regf, vpB=vpB):
                pg.wait(t, "vp", vpB + max(0, su - 1))
                t.matmul(regf[:, 0:512], ones1[:], biasq[0:1, 0:512],
                         start=True, stop=False)
                t.matmul(regf[:, 512:1024], ones1[:], biasq[0:1, 512:1024],
                         start=True, stop=False)
                for dcc in range(NC):
                    lh = xq_sb[:, dcc * 512 + su * 128: dcc * 512 + (su + 1) * 128]
                    t.matmul(regf[:, 0:512], lh, wq8_sb[:, dcc * D: dcc * D + 512],
                             start=False, stop=(dcc == NC - 1))
                    li = t.matmul(regf[:, 512:1024], lh,
                                  wq8_sb[:, dcc * D + 512:(dcc + 1) * D],
                                  start=False, stop=(dcc == NC - 1))
                pg.inc("tp", li)
            on("tensor", f_tf)

            def f_vf(v, ti=ti, su=su, regf=regf):
                pg.wait(v, "tp")
                xs = x_sb[:, (ti * NSUB + su) * D:(ti * NSUB + su + 1) * D]
                v.tensor_tensor(tr32, regf[:, 0:1024], g1_b[:], AT.mult)
                li = v.tensor_add(xs, xs, tr32)
                pg.inc("vp", li)
            on("vector", f_vf)

        def f_syo(sy, ti=ti):
            pg.wait(sy, "vp")
            d_out.inc(sy.dma_start(
                out=out_hbm.rearrange("(s p) d -> p s d", p=128)
                [:, ti * NSUB:(ti + 1) * NSUB, :],
                in_=x_sb[:, ti * NSUB * D:(ti + 1) * NSUB * D]))
        on("sync", f_syo)

    def f_syend(sy):
        for c0 in range(0, 8192, 64):
            d_out.inc(sy.dma_start(out=dbg_hbm[:, c0:c0 + 64], in_=dbg_fill[:])) \
                if False else None
        d_out.wait(sy)
    on("sync", f_syend)

    cm_exit(gbc_cm)
    cm_exit(xq_cm)
    cm_exit(g1_cm)
    cm_exit(wq8_cm)
    cm_exit(h2_cm)
    es.close()
    return nc


def _make_consts():
    import ml_dtypes
    cst_f = np.eye(128, dtype=np.float32)
    t = np.arange(128)
    p = np.arange(128)[:, None]
    rows = [np.eye(128, dtype=np.float32),
            (t[None, :] < p).astype(np.float32),
            (t[None, :] >= p).astype(np.float32)]
    bd = np.zeros((128, 128), dtype=np.float32)
    for c in range(NC):
        bd[0:64, c * 16 + 2 * c] = 1.0
        bd[64:128, c * 16 + 2 * c + 1] = 1.0
    rows.append(bd)
    ones = np.zeros((4, 128), dtype=np.float32)
    ones[0, :] = 1.0
    ones[1, :] = -1.0
    rows.append(ones)
    cst_b = np.concatenate(rows, axis=0)[0:516]
    mka = (1.0 - rows[1]) * -30000.0
    mkd = (1.0 - rows[2]) * -30000.0
    cst_b = np.concatenate([cst_b, mka, mkd], axis=0)
    return cst_f, np.ascontiguousarray(cst_b.astype(ml_dtypes.bfloat16))


_NC_CACHE = {}


def kernel(L0_attn_x, L0_attn_w_qkv, L0_attn_w_out, L0_ffn_weight, L0_ffn_bias,
           L0_c, L0_wmod):
    import os as _os
    stage = int(_os.environ.get("KSTAGE", "99"))
    n_tiles = L0_attn_x.shape[1] // TILE
    n_cores = L0_attn_x.shape[0]
    key = (n_tiles, n_cores, stage)
    if key not in _NC_CACHE:
        _NC_CACHE[key] = build_nc(n_tiles, n_cores, stage)
    nc = _NC_CACHE[key]

    x = np.ascontiguousarray(np.asarray(L0_attn_x, dtype=np.float32))
    wqkv = np.ascontiguousarray(np.asarray(L0_attn_w_qkv, dtype=np.float32))
    wout = np.ascontiguousarray(np.asarray(L0_attn_w_out, dtype=np.float32))
    wffn = np.ascontiguousarray(np.asarray(L0_ffn_weight, dtype=np.float32))
    fbias = np.ascontiguousarray(
        np.asarray(L0_ffn_bias, dtype=np.float32).reshape(1, D))
    c_all = np.ascontiguousarray(np.asarray(L0_c, dtype=np.float32))
    wmod = np.ascontiguousarray(np.asarray(L0_wmod, dtype=np.float32))
    cst_f, cst_b = _make_consts()

    in_maps = []
    for r in range(n_cores):
        in_maps.append({
            "x": np.ascontiguousarray(x[r]),
            "wqkv": wqkv,
            "wout": wout,
            "wffn": wffn,
            "fbias": fbias,
            "c_row": np.ascontiguousarray(c_all[r:r + 1]),
            "wmod": wmod,
            "cst_f": cst_f,
            "cst_b": cst_b,
        })
    res = run_bass_kernel_spmd(nc, in_maps, core_ids=list(range(n_cores)))
    out = np.stack([np.asarray(res.results[r]["out"])
                    for r in range(n_cores)], axis=0)
    return out
